# version 03 (final)
# baseline (speedup 1.0000x reference)
import contextlib
import math
import numpy as np
import ml_dtypes
import concourse.bass as bass
import concourse.mybir as mybir
from concourse.bass_utils import run_bass_kernel_spmd

F32 = mybir.dt.float32
BF16 = mybir.dt.bfloat16
AF = mybir.ActivationFunctionType
ALU = mybir.AluOpType
AX = mybir.AxisListType
BF = ml_dtypes.bfloat16

D = 1024
B = 4
S = 4096
T = 2048
NCH = 8
HEADS = 16
HD = 64
EPS = 1e-5
KB = 1024

ENGINES = ("tensor", "vector", "scalar", "gpsimd", "sync")


class Sem:
    __slots__ = ("h", "v")

    def __init__(self, h):
        self.h = h
        self.v = 0

    def inc(self, n=1):
        self.v += n
        return (self.h, n)

    def tok(self):
        return (self, self.v)


class Prog:
    def __init__(self, nc, es):
        self.nc = nc
        self.es = es
        self.q = {e: [] for e in ENGINES}
        self._nsem = 0
        self.sems = []
        self.pending = {}
        self.defer_replay = False

    def sem(self):
        h = self.es.enter_context(self.nc.semaphore(f"s{self._nsem}"))
        self._nsem += 1
        s = Sem(h)
        self.sems.append(s)
        return s

    def barrier(self):
        join = [(s, s.v) for s in self.sems if s.v > 0]
        for e in ENGINES:
            self.pending[e] = join

    def on(self, eng, fn, wait=(), inc=None):
        w = [(s.h, v) for (s, v) in wait]
        if eng in self.pending:
            w = [(s.h, v) for (s, v) in self.pending.pop(eng)] + w
        self.q[eng].append((w, fn, inc, FUSE.get("phase", 0) if FUSE.get("on") else 0))

    def replay(self, block):
        if self.defer_replay:
            return
        for eng in ENGINES:
            ops = self.q[eng]
            if not ops:
                continue

            def make(ops):
                def body(e):
                    for (waits, fn, inc, *_ph) in ops:
                        for (h, v) in waits:
                            e.wait_ge(h, v)
                        inst = fn(e)
                        if inc is not None:
                            inst.then_inc(inc[0], inc[1])
                return body

            getattr(block, eng)(make(ops))


class Bank:
    def __init__(self, ap):
        self.ap = ap
        self.free = []


FUSE = {"on": False, "phase": 0, "nphases": 8}
KERNEL_FUSED = True


class FusedNC:

    def __init__(self, nc):
        self._nc = nc

    def __getattr__(self, k):
        return getattr(self._nc, k)

    def dram_tensor(self, name, shape, dtype, kind="Internal"):
        ph = FUSE["phase"]
        if name == "x" and kind == "ExternalInput":
            return FUSE["chain_in"]
        if name == "y" and kind == "ExternalOutput":
            h = (self._nc.dram_tensor(f"p{ph}_y", shape, dtype, kind="Internal")
                 if ph < FUSE["nphases"] else
                 self._nc.dram_tensor("out", shape, dtype, kind="ExternalOutput"))
            FUSE["chain_out"] = h
            return h
        if name == "xo" and kind == "ExternalInput" and ph > 1:
            return FUSE["xo_pre"]
        if name == "halo" and kind == "ExternalInput" and FUSE.get("halo_pre") is not None:
            return FUSE["halo_pre"]
        return self._nc.dram_tensor(f"p{ph}_{name}", shape, dtype, kind=kind)

    def psum_tensor(self, name, shape, dtype=mybir.dt.float32, **kw):
        return self._nc.psum_tensor(f"p{FUSE['phase']}_{name}", shape, dtype, **kw)

    def Block(self):
        return contextlib.nullcontext(None)


def new_core(name):
    if FUSE["on"]:
        FUSE["p"].barrier()
        return FUSE["fnc"]
    nc = bass.Bass("TRN2", target_bir_lowering=False, name=name)
    return nc


SB_BASE = 16512


def sb_at(nc, name, shape, dtype, off):
    if FUSE["on"]:
        name = f"p{FUSE['phase']}_{name}"
    return nc.alloc_sbuf_tensor_at(name, shape, dtype, offset=SB_BASE + off)


def rearr_fm(ap):
    return ap.rearrange("(c p) t -> p c t", p=128)


def fm3(sb_ap, tlen):
    return sb_ap.rearrange("p (c t) -> p c t", t=tlen)



def emit_ln_tile(p, ctx, X, Xtt, t0, ttile, U, u0, Utt, extra_wait=(), u_guard=(), gb=None):
    sd, sa, sp = ctx["sd"], ctx["sa"], ctx["sp"]
    s_bf, sf32, stat = ctx["s_bf"], ctx["s_f32"], ctx["s_stat"]
    ps_A, ps_C = ctx["ps_A"], ctx["ps_C"]
    ew = list(extra_wait)

    mm_first_wait = list(ps_C.free)
    for c in range(NCH):
        xc = X[:, c * Xtt + t0: c * Xtt + t0 + ttile]
        sl0 = (c % 2) * 2 * ttile
        x2 = s_bf[:, sl0:sl0 + ttile]
        xb = s_bf[:, sl0 + ttile:sl0 + 2 * ttile]
        guard = [(sp, sp.v)] if c >= 2 else []
        p.on("scalar", lambda e, o=x2, i=xc: e.activation(out=o, in_=i, func=AF.Square),
             wait=guard + ew, inc=sa.inc())
        p.on("vector", lambda e, o=xb, i=xc: e.tensor_copy(o, i),
             wait=guard + ew, inc=sd.inc())
        p.on("tensor",
             lambda e, o=ps_C.ap[0:1, 0:ttile], l=ctx["ones_col"], r=xb, stt=(c == 0):
                 e.matmul(o, l, r, start=stt, stop=False),
             wait=[sd.tok()] + (mm_first_wait if c == 0 else []), inc=sp.inc())
        p.on("tensor",
             lambda e, o=ps_A.ap[0:1, 0:ttile], l=ctx["ones_col"], r=x2, stt=(c == 0), lst=(c == NCH - 1):
                 e.matmul(o, l, r, start=stt, stop=lst),
             wait=[sa.tok()] + (list(ps_A.free) if c == 0 else []), inc=sp.inc())
    u_r = stat[0:1, 0:ttile]
    t_r = stat[0:1, ttile:2 * ttile]
    var_r = stat[0:1, 2 * ttile:3 * ttile]
    rstd_r = stat[0:1, 3 * ttile:4 * ttile]
    p.on("vector", lambda e, o=u_r, i=ps_C.ap[0:1, 0:ttile]:
         e.tensor_scalar(out=o, in0=i, scalar1=1.0 / D, scalar2=None, op0=ALU.mult),
         wait=[sp.tok()], inc=sd.inc())
    p.on("vector", lambda e, o=t_r, a=u_r: e.tensor_mul(o, a, a),
         wait=[sd.tok()], inc=sd.inc())
    p.on("vector", lambda e, o=var_r, a=ps_A.ap[0:1, 0:ttile], b=t_r:
         e.scalar_tensor_tensor(out=o, in0=a, scalar=1.0 / D, in1=b,
                                op0=ALU.mult, op1=ALU.subtract),
         wait=[sd.tok()], inc=sd.inc())
    p.on("scalar", lambda e, o=rstd_r, i=var_r:
         e.activation(out=o, in_=i, func=AF.Ln, bias=ctx["eps_row"]),
         wait=[sd.tok()], inc=sa.inc())
    p.on("scalar", lambda e, o=rstd_r, i=rstd_r:
         e.activation(out=o, in_=i, func=AF.Exp, scale=-0.5),
         wait=[sa.tok()], inc=sa.inc())
    p.on("vector", lambda e, o=t_r, a=u_r, b=rstd_r: e.tensor_mul(o, a, b),
         wait=[sa.tok()], inc=sd.inc())
    p.on("tensor", lambda e, o=ps_A.ap[:, 0:ttile], l=ctx["ones_row"], r=rstd_r:
         e.matmul(o, l, r, start=True, stop=True),
         wait=[sa.tok()] + ps_A.free, inc=sp.inc())
    p.on("tensor", lambda e, o=ps_C.ap[:, 0:ttile], l=ctx["ones_row"], r=t_r:
         e.matmul(o, l, r, start=True, stop=True),
         wait=[sd.tok()] + ps_C.free, inc=sp.inc())
    bc_tok = sp.tok()
    ug = list(u_guard)
    for c in range(NCH):
        xc = X[:, c * Xtt + t0: c * Xtt + t0 + ttile]
        uc = U[:, c * Utt + u0: c * Utt + u0 + ttile]
        tm = sf32[:, (c % 2) * ttile:(c % 2) * ttile + ttile]
        guard = [(sd, sd.v)] if c >= 2 else []
        p.on("vector", lambda e, o=tm, a=xc, bb=ps_A.ap[:, 0:ttile]:
             e.tensor_mul(o, a, bb), wait=[bc_tok] + guard + ug, inc=sd.inc())
        if gb is None:
            p.on("vector", lambda e, o=uc, a=tm, bb=ps_C.ap[:, 0:ttile]:
                 e.tensor_sub(o, a, bb), wait=[sd.tok()], inc=sd.inc())
        else:
            g_sb, b_sb = gb
            p.on("vector", lambda e, o=tm, a=tm, bb=ps_C.ap[:, 0:ttile]:
                 e.tensor_sub(o, a, bb), wait=[sd.tok()], inc=sd.inc())
            p.on("vector", lambda e, o=uc, a=tm, c_=c:
                 e.tensor_scalar(out=o, in0=a, scalar1=g_sb[:, c_:c_ + 1],
                                 scalar2=b_sb[:, c_:c_ + 1], op0=ALU.mult, op1=ALU.add),
                 wait=[sd.tok()], inc=sd.inc())
    ps_A.free = [sd.tok()]
    ps_C.free = [sd.tok()]
    return sd.tok()




def emit_ln_rows(p, ctx, X, Xtt, t0, ttile, rows, Rtt, extra_wait=(), ln1=False, rt0=None):
    sd, sa, sp = ctx["sd"], ctx["sa"], ctx["sp"]
    s_bf, stat = ctx["s_bf"], ctx["s_stat"]
    ps_A, ps_C = ctx["ps_A"], ctx["ps_C"]
    ew = list(extra_wait)
    mm_first_wait = list(ps_C.free)
    for c in range(NCH):
        xc = X[:, c * Xtt + t0: c * Xtt + t0 + ttile]
        sl0 = (c % 2) * 2 * ttile
        x2 = s_bf[:, sl0:sl0 + ttile]
        xb = s_bf[:, sl0 + ttile:sl0 + 2 * ttile]
        guard = [(sp, sp.v)] if c >= 2 else []
        p.on("scalar", lambda e, o=x2, i=xc: e.activation(out=o, in_=i, func=AF.Square),
             wait=guard + ew, inc=sa.inc())
        p.on("vector", lambda e, o=xb, i=xc: e.tensor_copy(o, i),
             wait=guard + ew, inc=sd.inc())
        p.on("tensor",
             lambda e, o=ps_C.ap[0:1, 0:ttile], l=ctx["ones_col"], r=xb, stt=(c == 0):
                 e.matmul(o, l, r, start=stt, stop=False),
             wait=[sd.tok()] + (mm_first_wait if c == 0 else []), inc=sp.inc())
        p.on("tensor",
             lambda e, o=ps_A.ap[0:1, 0:ttile], l=ctx["ones_col"], r=x2, stt=(c == 0), lst=(c == NCH - 1):
                 e.matmul(o, l, r, start=stt, stop=lst),
             wait=[sa.tok()] + (list(ps_A.free) if c == 0 else []), inc=sp.inc())
    u_r = stat[0:1, 0:ttile]
    t_r = stat[0:1, ttile:2 * ttile]
    var_r = stat[0:1, 2 * ttile:3 * ttile]
    if rt0 is None:
        rt0 = t0
    rstd_s = stat[0:1, 3 * ttile:4 * ttile]
    p.on("vector", lambda e, o=u_r, i=ps_C.ap[0:1, 0:ttile]:
         e.tensor_scalar(out=o, in0=i, scalar1=1.0 / D, scalar2=None, op0=ALU.mult),
         wait=[sp.tok()], inc=sd.inc())
    p.on("vector", lambda e, o=t_r, a=u_r: e.tensor_mul(o, a, a),
         wait=[sd.tok()], inc=sd.inc())
    p.on("vector", lambda e, o=var_r, a=ps_A.ap[0:1, 0:ttile], b=t_r:
         e.scalar_tensor_tensor(out=o, in0=a, scalar=1.0 / D, in1=b,
                                op0=ALU.mult, op1=ALU.subtract),
         wait=[sd.tok()], inc=sd.inc())
    ps_A.free = [sd.tok()]
    ps_C.free = [sd.tok()]
    p.on("scalar", lambda e, o=rstd_s, i=var_r:
         e.activation(out=o, in_=i, func=AF.Ln, bias=ctx["eps_row"]),
         wait=[sd.tok()], inc=sa.inc())
    p.on("scalar", lambda e, o=rstd_s, i=rstd_s:
         e.activation(out=o, in_=i, func=AF.Exp, scale=-0.5),
         wait=[sa.tok()], inc=sa.inc())
    p.on("scalar", lambda e, o=rows[0:1, rt0:rt0 + ttile], i=rstd_s:
         e.activation(out=o, in_=i, func=AF.Identity),
         wait=[sa.tok()], inc=sa.inc())
    cpre_s = t_r
    p.on("vector", lambda e, o=cpre_s, a=u_r, b=rstd_s: e.tensor_mul(o, a, b),
         wait=[sa.tok()], inc=sd.inc())
    p.on("scalar", lambda e, o=rows[32:33, rt0:rt0 + ttile], i=cpre_s:
         e.activation(out=o, in_=i, func=AF.Identity),
         wait=[sd.tok()], inc=sa.inc())
    if not ln1:
        return sa.tok()
    w_r = u_r
    p.on("vector", lambda e, o=w_r, a=rstd_s: e.tensor_mul(o, a, a),
         wait=[(sa, sa.v), (sd, sd.v)], inc=sd.inc())
    p.on("vector", lambda e, o=w_r, a=var_r, b=w_r: e.tensor_mul(o, a, b),
         wait=[sd.tok()], inc=sd.inc())
    p.on("scalar", lambda e, o=w_r, i=w_r:
         e.activation(out=o, in_=i, func=AF.Ln, bias=ctx["eps_row"]),
         wait=[sd.tok()], inc=sa.inc())
    p.on("scalar", lambda e, o=w_r, i=w_r:
         e.activation(out=o, in_=i, func=AF.Exp, scale=-0.5),
         wait=[sa.tok()], inc=sa.inc())
    rstd1_s = var_r
    p.on("vector", lambda e, o=rstd1_s, a=rstd_s, b=w_r: e.tensor_mul(o, a, b),
         wait=[sa.tok()], inc=sd.inc())
    p.on("scalar", lambda e, o=rows[64:65, rt0:rt0 + ttile], i=rstd1_s:
         e.activation(out=o, in_=i, func=AF.Identity),
         wait=[sd.tok()], inc=sa.inc())
    p.on("vector", lambda e, o=rows[0:1, Rtt + rt0: Rtt + rt0 + ttile], a=cpre_s, b=w_r:
         e.tensor_mul(o, a, b), wait=[sd.tok()], inc=sd.inc())
    return sa.tok() if False else (lambda: sd.tok())()


def emit_ternarize(p, ctx, w_dram, wt_dram, R, C):
    sd, sa, sp, sl, st = ctx["sd"], ctx["sa"], ctx["sp"], ctx["sl"], ctx["st"]
    wf = ctx["tern_f32"]
    sgn = ctx["tern_bf"]
    acc = ctx["s_stat"]
    ps = ctx["ps_misc"]
    nch = R // 128
    wr = w_dram.rearrange("(c p) m -> c p m", p=128)
    wtr = wt_dram.rearrange("(c p) m -> c p m", p=128)

    red_toks = []
    for k in range(nch):
        buf = wf[:, (k % 2) * C:(k % 2) * C + C]
        guard = [red_toks[k - 2]] if k >= 2 else list(ctx.get("tern_buf_free", []))
        p.on("sync", lambda e, o=buf, i=wr[k]: e.dma_start(out=o, in_=i),
             wait=guard, inc=sl.inc(16))
        p.on("vector", lambda e, o=acc[:, k:k + 1], i=buf:
             e.tensor_reduce(out=o, in_=i, axis=AX.X, op=ALU.add, apply_absolute_value=True),
             wait=[sl.tok()], inc=sd.inc())
        red_toks.append(sd.tok())
    p.on("vector", lambda e, o=acc[:, nch:nch + 1], i=acc[:, 0:nch]:
         e.tensor_reduce(out=o, in_=i, axis=AX.X, op=ALU.add),
         wait=[sd.tok()], inc=sd.inc())
    p.on("tensor", lambda e, o=ps.ap[0:1, 0:1], l=ctx["ones_f32col"], r=acc[:, nch:nch + 1]:
         e.matmul(o, l, r, start=True, stop=True),
         wait=[sd.tok()] + ps.free, inc=sp.inc())
    th_r = acc[0:1, nch + 1:nch + 2]
    p.on("vector", lambda e, o=th_r, i=ps.ap[0:1, 0:1]:
         e.tensor_scalar(out=o, in0=i, scalar1=0.5 / (R * C), scalar2=0.5e-8,
                         op0=ALU.mult, op1=ALU.max),
         wait=[sp.tok()], inc=sd.inc())
    p.on("tensor", lambda e, o=ps.ap[:, 4:5], l=ctx["ones_row"], r=th_r:
         e.matmul(o, l, r, start=True, stop=True),
         wait=[sd.tok()], inc=sp.inc())
    thb = acc[:, nch + 2:nch + 3]
    thn = acc[:, nch + 3:nch + 4]
    p.on("vector", lambda e, o=thb, i=ps.ap[:, 4:5]: e.tensor_copy(o, i),
         wait=[sp.tok()], inc=sd.inc())
    p.on("vector", lambda e, o=thn, i=thb:
         e.tensor_scalar(out=o, in0=i, scalar1=-1.0, scalar2=None, op0=ALU.mult),
         wait=[sd.tok()], inc=sd.inc())
    ps.free = [sd.tok()]
    th_tok = sd.tok()

    store_toks = []
    for k in range(nch):
        buf = wf[:, (k % 2) * C:(k % 2) * C + C]
        s1 = sgn[:, 0:C]
        s2 = sgn[:, C:2 * C]
        out = sgn[:, 2 * C:3 * C]
        dma_guard = [th_tok] if k < 2 else [store_toks[k - 2][0]]
        p.on("sync", lambda e, o=buf, i=wr[k]: e.dma_start(out=o, in_=i),
             wait=dma_guard, inc=sl.inc(16))
        act_guard = [sl.tok()] + ([store_toks[k - 1][1]] if k >= 1 else [th_tok])
        p.on("scalar", lambda e, o=s1, i=buf:
             e.activation(out=o, in_=i, func=AF.Sign, bias=thn[:, 0:1]),
             wait=act_guard, inc=sa.inc())
        p.on("scalar", lambda e, o=s2, i=buf:
             e.activation(out=o, in_=i, func=AF.Sign, bias=thb[:, 0:1]),
             wait=[sa.tok()], inc=sa.inc())
        add_guard = [sa.tok()] + ([store_toks[k - 1][0]] if k >= 1 else [])
        p.on("vector", lambda e, o=out, a=s1, b=s2: e.tensor_add(o, a, b),
             wait=add_guard, inc=sd.inc())
        p.on("sync", lambda e, o=wtr[k], i=out: e.dma_start(out=o, in_=i),
             wait=[sd.tok()], inc=st.inc(16))
        store_toks.append((st.tok(), sd.tok()))
    ctx["tern_buf_free"] = [store_toks[-1][0], store_toks[-2][0]]
    return st.tok()


def shared_prog(nc, es):
    if FUSE["on"]:
        return FUSE["p"]
    return Prog(nc, es)


def make_common(nc, es, p, *, scr_off):
    if FUSE["on"] and FUSE.get("ctx") is not None:
        return FUSE["ctx"]
    s_bf = sb_at(nc, "s_bf", [128, 4 * 512], BF16, scr_off)
    s_f32 = sb_at(nc, "s_f32", [128, 2 * 512], F32, scr_off + 4 * KB)
    s_stat = sb_at(nc, "s_stat", [128, 2048], F32, scr_off + 8 * KB)
    consts = sb_at(nc, "consts", [128, 8], F32, scr_off + 16 * KB)
    constb = sb_at(nc, "constb", [128, 16], BF16, scr_off + 16 * KB + 64)
    ones_row = sb_at(nc, "ones_row", [1, 128], F32, scr_off + 16 * KB + 128)
    ctx = {
        "s_bf": s_bf[:, :], "s_f32": s_f32[:, :], "s_stat": s_stat[:, :],
        "ones_col": constb[:, 0:1], "ones_row": ones_row[0:1, :],
        "ones_f32col": consts[:, 0:1],
        "scr_end": scr_off + 16 * KB + 128 + 512,
    }
    sg = p.sem()
    p.on("gpsimd", lambda e: e.memset(consts[:, :], 1.0), inc=sg.inc())
    p.on("gpsimd", lambda e: e.memset(constb[:, :], 1.0), inc=sg.inc())
    p.on("gpsimd", lambda e: e.memset(ones_row[0:1, :], 1.0), inc=sg.inc())
    p.on("gpsimd", lambda e: e.memset(consts[0:1, 1:2], EPS), inc=sg.inc())
    ctx["eps_row"] = consts[0:1, 1:2]
    ctx["sg"] = sg
    ctx["const_tok"] = (sg, sg.v)
    for name in ("sd", "sa", "sp", "sl", "st", "sw"):
        ctx[name] = p.sem()
    if FUSE["on"]:
        FUSE["ctx"] = ctx
    return ctx



DEBUG_L1FFN = False
L1FFN_STAGE = 0


def build_l1_ffn():
    nc = new_core("l1ffn")
    x_d = nc.dram_tensor("x", [D, T], F32, kind="ExternalInput").ap()
    if DEBUG_L1FFN:
        u_dbg = nc.dram_tensor("u_dbg", [D, 512], BF16, kind="ExternalOutput").ap()
        h_dbg = nc.dram_tensor("h_dbg", [4 * D, 512], BF16, kind="ExternalOutput").ap()
        w2s_dbg = nc.dram_tensor("w2s_dbg", [128, 4096], BF16, kind="ExternalOutput").ap()
        x_dbg = nc.dram_tensor("x_dbg", [128, 512], F32, kind="ExternalOutput").ap()
        tmp_dbg = nc.dram_tensor("tmp_dbg", [128, 512], F32, kind="ExternalOutput").ap()
        wt2_dbg = nc.dram_tensor("wt2_dbg", [4 * D, D], BF16, kind="ExternalOutput").ap()
    w1t_d = nc.dram_tensor("w1t", [D, 4 * D], F32, kind="ExternalInput").ap()
    w2t_d = nc.dram_tensor("w2t", [4 * D, D], F32, kind="ExternalInput").ap()
    b1_d = nc.dram_tensor("b1r", [128, 32], F32, kind="ExternalInput").ap()
    b2_d = nc.dram_tensor("b2r", [128, 8], F32, kind="ExternalInput").ap()
    y_d = nc.dram_tensor("y", [D, T], F32, kind="ExternalOutput").ap()
    wt1_d = nc.dram_tensor("wt1", [D, 4 * D], BF16, kind="Internal").ap()
    wt2_d = nc.dram_tensor("wt2", [4 * D, D], BF16, kind="Internal").ap()

    es = contextlib.ExitStack()
    with es:
        X = sb_at(nc, "X", [128, NCH * T], F32, 0)
        W1 = sb_at(nc, "W1", [128, NCH * 4 * D], BF16, 64 * KB)
        tern_f32 = sb_at(nc, "tern_f32", [128, 2 * 4096], F32, 128 * KB)
        tern_bf = sb_at(nc, "tern_bf", [128, 3 * 4096], BF16, 160 * KB)
        U = sb_at(nc, "U", [128, NCH * 512], BF16, 128 * KB)
        H = sb_at(nc, "H", [128, 32 * 512], BF16, 136 * KB)
        W2s = sb_at(nc, "W2s", [128, 2 * 32 * 128], BF16, 168 * KB)
        b1_sb = sb_at(nc, "b1_sb", [128, 32], F32, 184 * KB)
        b2_sb = sb_at(nc, "b2_sb", [128, 8], F32, 184 * KB + 256)
        ps = [Bank(es.enter_context(nc.psum_tensor(f"ps{i}", [128, 512], F32)))
              for i in range(8)]
        p = shared_prog(nc, es)
        ctx = make_common(nc, es, p, scr_off=186 * KB)
        sd, sa, sp = ctx["sd"], ctx["sa"], ctx["sp"]
        sl, st, sw, sg = ctx["sl"], ctx["st"], ctx["sw"], ctx["sg"]
        ctx["ps_stat"], ctx["ps_A"], ctx["ps_C"], ctx["ps_misc"] = ps[0], ps[1], ps[2], ps[3]
        ctx["tern_f32"], ctx["tern_bf"] = tern_f32[:, :], tern_bf[:, :]

        p.on("scalar", lambda e: e.dma_start(out=fm3(X[:, :], T), in_=rearr_fm(x_d)), inc=sl.inc(16))
        x_tok = (sl, sl.v)
        p.on("scalar", lambda e: e.dma_start(out=b1_sb[:, :], in_=b1_d), inc=sl.inc(16))
        p.on("scalar", lambda e: e.dma_start(out=b2_sb[:, :], in_=b2_d), inc=sl.inc(16))
        bias_tok = (sl, sl.v)

        emit_ternarize(p, ctx, w1t_d, wt1_d, D, 4 * D)
        t1_tok = (ctx["st"], ctx["st"].v)
        emit_ternarize(p, ctx, w2t_d, wt2_d, 4 * D, D)
        t2_tok = (ctx["st"], ctx["st"].v)

        p.on("scalar", lambda e: e.dma_start(
            out=fm3(W1[:, :], 4 * D), in_=wt1_d.rearrange("(c p) m -> p c m", p=128)),
            wait=[t1_tok], inc=sl.inc(16))
        w1_tok = (sl, sl.v)

        up_banks = (ps[4], ps[5])
        dn_banks = (ps[6], ps[7])
        w2sl = wt2_d
        nw2 = 0
        for tt in range(T // 512):
            t0 = tt * 512
            uready = emit_ln_tile(
                p, ctx, X[:, :], T, t0, 512, U[:, :], 0, 512,
                extra_wait=[x_tok, t2_tok, ctx["const_tok"]],
                u_guard=[(sp, sp.v)])
            for m in range(32):
                bank = up_banks[m % 2]
                for k in range(NCH):
                    w_ap = W1[:, k * 4 * D + m * 128: k * 4 * D + (m + 1) * 128]
                    u_ap = U[:, k * 512:(k + 1) * 512]
                    wait = ([uready, w1_tok, bias_tok] + bank.free) if k == 0 else []
                    p.on("tensor",
                         lambda e, o=bank.ap[:, :], l=w_ap, r=u_ap, stt=(k == 0), lst=(k == NCH - 1):
                             e.matmul(o, l, r, start=stt, stop=lst),
                         wait=wait, inc=sp.inc())
                p.on("scalar",
                     lambda e, o=H[:, m * 512:(m + 1) * 512], i=bank.ap[:, :], m_=m:
                         e.activation(out=o, in_=i, func=AF.Gelu, scale=0.5,
                                      bias=b1_sb[:, m_:m_ + 1]),
                     wait=[sp.tok()], inc=sa.inc())
                bank.free = [sa.tok()]
            h_tok = sa.tok()
            if DEBUG_L1FFN and tt == 0:
                p.on("sync", lambda e: e.dma_start(out=rearr_fm(u_dbg), in_=fm3(U[:, :], 512)),
                     wait=[uready], inc=ctx["st"].inc(16))
                p.on("sync", lambda e: e.dma_start(out=rearr_fm(h_dbg), in_=fm3(H[:, :], 512)),
                     wait=[h_tok], inc=ctx["st"].inc(16))
            for d in range(NCH):
                slot = (tt * NCH + d) % 2
                w2slice = w2sl[:, d * 128:(d + 1) * 128].rearrange("(m p) q -> p m q", p=128)
                dst = W2s[:, slot * 4096:(slot + 1) * 4096].rearrange("p (m q) -> p m q", q=128)
                p.on("scalar", lambda e, o=dst, i=w2slice: e.dma_start(out=o, in_=i),
                     wait=[(sp, sp.v)] if nw2 >= 2 else [t2_tok], inc=sw.inc(16))
                w2_tok = (sw, sw.v)
                nw2 += 1
                bank = dn_banks[d % 2]
                for m in range(32):
                    w_ap = W2s[:, slot * 4096 + m * 128: slot * 4096 + (m + 1) * 128]
                    h_ap = H[:, m * 512:(m + 1) * 512]
                    wait = ([h_tok, w2_tok] + bank.free) if m == 0 else []
                    p.on("tensor",
                         lambda e, o=bank.ap[:, :], l=w_ap, r=h_ap, stt=(m == 0), lst=(m == 31):
                             e.matmul(o, l, r, start=stt, stop=lst),
                         wait=wait, inc=sp.inc())
                if DEBUG_L1FFN and tt == 0 and d == 0:
                    p.on("sync", lambda e: e.dma_start(out=w2s_dbg, in_=W2s[:, 0:4096]),
                         wait=[(sw, 16 * nw2)], inc=ctx["st"].inc(16))
                tmp = ctx["s_f32"][:, (d % 2) * 512:(d % 2) * 512 + 512]
                xc = X[:, d * T + t0: d * T + t0 + 512]
                p.on("vector",
                     lambda e, o=tmp, i=bank.ap[:, :], x_=xc:
                         e.scalar_tensor_tensor(out=o, in0=i, scalar=0.5, in1=x_,
                                                op0=ALU.mult, op1=ALU.add),
                     wait=[sp.tok(), (sa, sa.v)], inc=sd.inc())
                bank.free = [sd.tok()]
                id_wait = [sd.tok()]
                if DEBUG_L1FFN and tt == 0 and d == 0:
                    p.on("sync", lambda e, i=tmp: e.dma_start(out=tmp_dbg, in_=i),
                         wait=[sd.tok()], inc=ctx["st"].inc(16))
                    id_wait.append((ctx["st"], ctx["st"].v))
                p.on("scalar",
                     lambda e, o=xc, i=tmp, d_=d:
                         e.activation(out=o, in_=i, func=AF.Identity, bias=b2_sb[:, d_:d_ + 1]),
                     wait=id_wait, inc=sa.inc())
        if DEBUG_L1FFN:
            p.on("sync", lambda e: e.dma_start(out=x_dbg, in_=X[:, 0:512]),
                 wait=[sa.tok()], inc=ctx["st"].inc(16))
        fin = sa.tok()
        p.on("sync", lambda e: e.dma_start(out=rearr_fm(y_d), in_=fm3(X[:, :], T)),
             wait=[fin], inc=st.inc(16))
        if DEBUG_L1FFN:
            for c in range(32):
                p.on("gpsimd", lambda e, c_=c: e.dma_start(
                    out=H[:, 0:1024] if False else H[:, (c % 2) * 1024:(c % 2) * 1024 + 1024],
                    in_=wt2_d.rearrange("(c p) m -> c p m", p=128)[c_]),
                    wait=[fin, (st, st.v)], inc=sl.inc(16))
                p.on("sync", lambda e, c_=c: e.dma_start(
                    out=wt2_dbg.rearrange("(c p) m -> c p m", p=128)[c_],
                    in_=H[:, (c % 2) * 1024:(c % 2) * 1024 + 1024]),
                    wait=[(sl, sl.v)], inc=st.inc(16))
        p.on("gpsimd", lambda e: e.engine_nop(), wait=[st.tok()])

        block = es.enter_context(nc.Block())
        p.replay(block)
    return nc


_L1FFN_NC = None


def run_l1_ffn(x_full, W1, b1, W2, b2):
    global _L1FFN_NC
    if _L1FFN_NC is None:
        _L1FFN_NC = build_l1_ffn()
    w1t = np.ascontiguousarray(W1.T)
    w2t = np.ascontiguousarray(W2.T)
    b1r = np.ascontiguousarray(b1.reshape(32, 128).T)
    b2r = np.ascontiguousarray(b2.reshape(8, 128).T)
    in_maps = []
    for core in range(8):
        b, h = divmod(core, 2)
        xs = np.ascontiguousarray(x_full[b, h * T:(h + 1) * T, :].T)
        in_maps.append({"x": xs, "w1t": w1t, "w2t": w2t, "b1r": b1r, "b2r": b2r})
    res = run_bass_kernel_spmd(_L1FFN_NC, in_maps, core_ids=list(range(8)))
    out = np.empty((B, S, D), np.float32)
    for core in range(8):
        b, h = divmod(core, 2)
        out[b, h * T:(h + 1) * T, :] = res.results[core]["y"].T
    return out





def emit_cast_load(p, ctx, w_dram, Wsb, nch, C, stage, dstride=None, doff=0):
    sl, sa = ctx["sl"], ctx["sa"]
    if dstride is None:
        dstride = C
    wr = w_dram.rearrange("(c p) m -> c p m", p=128)
    toks = []
    for k in range(nch):
        buf = stage[:, (k % 2) * C:(k % 2) * C + C]
        guard = [toks[k - 2]] if k >= 2 else list(ctx.get("cast_stage_free", []))
        p.on("sync", lambda e, o=buf, i=wr[k]: e.dma_start(out=o, in_=i),
             wait=guard, inc=sl.inc(16))
        p.on("scalar", lambda e, o=Wsb[:, k * dstride + doff: k * dstride + doff + C], i=buf:
             e.activation(out=o, in_=i, func=AF.Identity),
             wait=[sl.tok()], inc=sa.inc())
        toks.append(sa.tok())
    ctx["cast_stage_free"] = [toks[-1], toks[-2]] if nch >= 2 else [toks[-1]]
    return sa.tok()


DEBUG_L0FFN = False


def build_l0_ffn():
    nc = new_core("l0ffn")
    TT1 = T + 1
    x_d = nc.dram_tensor("x", [D, T], F32, kind="ExternalInput").ap()
    halo_d = nc.dram_tensor("halo", [D, 1], F32, kind="ExternalInput").ap()
    wr_d = nc.dram_tensor("wrt", [D, D], F32, kind="ExternalInput").ap()
    wk_d = nc.dram_tensor("wkt", [D, D], F32, kind="ExternalInput").ap()
    wv_d = nc.dram_tensor("wvt", [D, D], F32, kind="ExternalInput").ap()
    mur_d = nc.dram_tensor("mur", [128, 8], F32, kind="ExternalInput").ap()
    muk_d = nc.dram_tensor("muk", [128, 8], F32, kind="ExternalInput").ap()
    y_d = nc.dram_tensor("y", [D, T], F32, kind="ExternalOutput").ap()
    if DEBUG_L0FFN:
        xr_dbg = nc.dram_tensor("xr_dbg", [D, 512], BF16, kind="ExternalOutput").ap()
        u_dbg = nc.dram_tensor("u_dbg", [D, 520], BF16, kind="ExternalOutput").ap()
        r_dbg = nc.dram_tensor("r_dbg", [D, 512], BF16, kind="ExternalOutput").ap()
        ka_dbg = nc.dram_tensor("ka_dbg", [D, 512], BF16, kind="ExternalOutput").ap()

    es = contextlib.ExitStack()
    with es:
        X = sb_at(nc, "X", [128, NCH * TT1], F32, 0)
        U = sb_at(nc, "U", [128, NCH * TT1], BF16, 66 * KB)
        WR = sb_at(nc, "WR", [128, NCH * D], BF16, 100 * KB)
        WK = sb_at(nc, "WK", [128, NCH * D], BF16, 116 * KB)
        WV = sb_at(nc, "WV", [128, NCH * D], BF16, 132 * KB)
        wstage = sb_at(nc, "wstage", [128, 2 * D], F32, 148 * KB)
        Rt = sb_at(nc, "Rt", [128, NCH * 512], BF16, 156 * KB)
        KA = sb_at(nc, "KA", [128, NCH * 512], BF16, 164 * KB)
        XR = sb_at(nc, "XR", [128, NCH * 512], BF16, 172 * KB)
        XK = sb_at(nc, "XK", [128, NCH * 512], BF16, 180 * KB)
        mur_sb = sb_at(nc, "mur_sb", [128, 8], F32, 188 * KB)
        muk_sb = sb_at(nc, "muk_sb", [128, 8], F32, 188 * KB + 64)
        ps = [Bank(es.enter_context(nc.psum_tensor(f"ps{i}", [128, 512], F32)))
              for i in range(6)]
        p = shared_prog(nc, es)
        ctx = make_common(nc, es, p, scr_off=189 * KB)
        sd, sa, sp = ctx["sd"], ctx["sa"], ctx["sp"]
        sl, st = ctx["sl"], ctx["st"]
        ctx["ps_stat"], ctx["ps_A"], ctx["ps_C"] = ps[0], ps[1], ps[2]

        p.on("sync", lambda e: e.dma_start(
            out=X[:, :].rearrange("p (c t) -> p c t", t=TT1)[:, :, 1:TT1],
            in_=rearr_fm(x_d)), inc=sl.inc(16))
        p.on("sync", lambda e: e.dma_start(
            out=X[:, :].rearrange("p (c t) -> p c t", t=TT1)[:, :, 0:1],
            in_=halo_d.rearrange("(c p) t -> p c t", p=128),
            allow_slow_non_contiguous=True), inc=sl.inc(16))
        p.on("sync", lambda e: e.dma_start(out=mur_sb[:, :], in_=mur_d), inc=sl.inc(16))
        p.on("sync", lambda e: e.dma_start(out=muk_sb[:, :], in_=muk_d), inc=sl.inc(16))
        x_tok = (sl, sl.v)
        w_toks = []
        for wd, Wsb in ((wr_d, WR), (wk_d, WK), (wv_d, WV)):
            w_toks.append(emit_cast_load(p, ctx, wd, Wsb[:, :], NCH, D, wstage[:, :]))

        ctx["x_ready"] = [x_tok, ctx["const_tok"]]
        emit_ln_tile(p, ctx, X[:, :], TT1, 0, 1, U[:, :], 0, TT1,
                     extra_wait=ctx["x_ready"])
        for tt in range(T // 512):
            emit_ln_tile(p, ctx, X[:, :], TT1, 1 + tt * 512, 512, U[:, :], 1 + tt * 512, TT1,
                         extra_wait=ctx["x_ready"])
        uready = (sd, sd.v)

        pb = (ps[3], ps[4])
        dump_fence = []
        for tt in range(T // 512):
            t0 = 1 + tt * 512
            for c in range(NCH):
                ucur = U[:, c * TT1 + t0: c * TT1 + t0 + 512]
                uprev = U[:, c * TT1 + t0 - 1: c * TT1 + t0 - 1 + 512]
                tsub = ctx["s_bf"][:, (c % 2) * 512:(c % 2) * 512 + 512]
                guard = [(sp, sp.v)] if (tt > 0 and c < 2) else []
                p.on("vector", lambda e, o=tsub, a=ucur, b=uprev: e.tensor_sub(o, a, b),
                     wait=[uready] + ([(sd, sd.v)] if c >= 2 else []) + guard + dump_fence,
                     inc=sd.inc())
                p.on("vector", lambda e, o=XR[:, c * 512:(c + 1) * 512], a=tsub, up=uprev, c_=c:
                     e.scalar_tensor_tensor(out=o, in0=a, scalar=mur_sb[:, c_:c_ + 1],
                                            in1=up, op0=ALU.mult, op1=ALU.add),
                     wait=[sd.tok()], inc=sd.inc())
                p.on("vector", lambda e, o=XK[:, c * 512:(c + 1) * 512], a=tsub, up=uprev, c_=c:
                     e.scalar_tensor_tensor(out=o, in0=a, scalar=muk_sb[:, c_:c_ + 1],
                                            in1=up, op0=ALU.mult, op1=ALU.add),
                     wait=[sd.tok()], inc=sd.inc())
            xrk_tok = sd.tok()
            for (Wsb, OUT, wtok, kind) in ((WR, Rt, w_toks[0], "r"), (WK, KA, w_toks[1], "k")):
                for m in range(NCH):
                    bank = pb[m % 2]
                    for k in range(NCH):
                        w_ap = Wsb[:, k * D + m * 128: k * D + (m + 1) * 128]
                        rhs = (XR if kind == "r" else XK)[:, k * 512:(k + 1) * 512]
                        wait = ([xrk_tok, wtok] + bank.free) if k == 0 else []
                        p.on("tensor", lambda e, o=bank.ap[:, :], l=w_ap, r=rhs,
                             s_=(k == 0), e_=(k == NCH - 1): e.matmul(o, l, r, start=s_, stop=e_),
                             wait=wait, inc=sp.inc())
                    if kind == "r":
                        p.on("scalar", lambda e, o=OUT[:, m * 512:(m + 1) * 512], i=bank.ap[:, :]:
                             e.activation(out=o, in_=i, func=AF.Sigmoid),
                             wait=[sp.tok()], inc=sa.inc())
                        bank.free = [sa.tok()]
                    else:
                        tb = ctx["s_bf"][:, 1024 + (m % 2) * 512: 1024 + (m % 2) * 512 + 512]
                        p.on("vector", lambda e, o=tb, i=bank.ap[:, :]:
                             e.tensor_scalar(out=o, in0=i, scalar1=0.0, scalar2=None, op0=ALU.max),
                             wait=[sp.tok()] + ([(sa, sa.v)] if m >= 2 else []), inc=sd.inc())
                        bank.free = [sd.tok()]
                        p.on("scalar", lambda e, o=OUT[:, m * 512:(m + 1) * 512], i=tb:
                             e.activation(out=o, in_=i, func=AF.Square),
                             wait=[sd.tok()], inc=sa.inc())
            rk_tok = sa.tok()
            if DEBUG_L0FFN and tt == 0:
                p.on("sync", lambda e: e.dma_start(
                    out=rearr_fm(u_dbg),
                    in_=U[:, :].rearrange("p (c t) -> p c t", t=TT1)[:, :, 0:520]),
                     wait=[xrk_tok], inc=st.inc(16))
                p.on("sync", lambda e: e.dma_start(out=rearr_fm(xr_dbg), in_=fm3(XR[:, :], 512)),
                     wait=[xrk_tok], inc=st.inc(16))
                p.on("sync", lambda e: e.dma_start(out=rearr_fm(r_dbg), in_=fm3(Rt[:, :], 512)),
                     wait=[rk_tok], inc=st.inc(16))
                p.on("sync", lambda e: e.dma_start(out=rearr_fm(ka_dbg), in_=fm3(KA[:, :], 512)),
                     wait=[rk_tok], inc=st.inc(16))
                dump_fence.append((st, st.v))
            for m in range(NCH):
                bank = pb[m % 2]
                for k in range(NCH):
                    w_ap = WV[:, k * D + m * 128: k * D + (m + 1) * 128]
                    rhs = KA[:, k * 512:(k + 1) * 512]
                    wait = ([rk_tok, w_toks[2]] + bank.free) if k == 0 else []
                    p.on("tensor", lambda e, o=bank.ap[:, :], l=w_ap, r=rhs,
                         s_=(k == 0), e_=(k == NCH - 1): e.matmul(o, l, r, start=s_, stop=e_),
                         wait=wait, inc=sp.inc())
                tv = ctx["s_f32"][:, (m % 2) * 512:(m % 2) * 512 + 512]
                xc = X[:, m * TT1 + t0: m * TT1 + t0 + 512]
                p.on("vector", lambda e, o=tv, a=Rt[:, m * 512:(m + 1) * 512], b=bank.ap[:, :]:
                     e.tensor_mul(o, a, b),
                     wait=[sp.tok(), (sd, sd.v)], inc=sd.inc())
                bank.free = [sd.tok()]
                p.on("vector", lambda e, o=xc, a=tv, b=xc: e.tensor_add(o, a, b),
                     wait=[sd.tok()], inc=sd.inc())
        fin = (sd, sd.v)
        p.on("sync", lambda e: e.dma_start(
            out=rearr_fm(y_d),
            in_=X[:, :].rearrange("p (c t) -> p c t", t=TT1)[:, :, 1:TT1]),
            wait=[fin], inc=st.inc(16))
        p.on("gpsimd", lambda e: e.engine_nop(), wait=[st.tok()])
        block = es.enter_context(nc.Block())
        p.replay(block)
    return nc


_L0FFN_NC = None


def run_l0_ffn(x_full, W_r, W_k, W_v, mu_r, mu_k):
    global _L0FFN_NC
    if _L0FFN_NC is None:
        _L0FFN_NC = build_l0_ffn()
    wrt = np.ascontiguousarray(W_r.T)
    wkt = np.ascontiguousarray(W_k.T)
    wvt = np.ascontiguousarray(W_v.T)
    mur = np.ascontiguousarray(mu_r.reshape(8, 128).T)
    muk = np.ascontiguousarray(mu_k.reshape(8, 128).T)
    in_maps = []
    for core in range(8):
        b, h = divmod(core, 2)
        xs = np.ascontiguousarray(x_full[b, h * T:(h + 1) * T, :].T)
        halo = (np.zeros((D, 1), np.float32) if h == 0
                else np.ascontiguousarray(x_full[b, h * T - 1, :].reshape(D, 1)))
        in_maps.append({"x": xs, "halo": halo, "wrt": wrt, "wkt": wkt, "wvt": wvt,
                        "mur": mur, "muk": muk})
    res = run_bass_kernel_spmd(_L0FFN_NC, in_maps, core_ids=list(range(8)))
    out = np.empty((B, S, D), np.float32)
    for core in range(8):
        b, h = divmod(core, 2)
        out[b, h * T:(h + 1) * T, :] = res.results[core]["y"].T
    return out





L1ATTB_STAGE = 0


def build_l1_attn_b():
    nc = new_core("l1attb" + str(L1ATTB_STAGE))
    x_d = nc.dram_tensor("x", [D, T], F32, kind="ExternalInput").ap()
    xo_d = nc.dram_tensor("xo", [D, T], F32, kind="ExternalInput").ap()
    wq_d = nc.dram_tensor("wq", [D, D], F32, kind="ExternalInput").ap()
    wk_d = nc.dram_tensor("wk", [D, D], F32, kind="ExternalInput").ap()
    wv_d = nc.dram_tensor("wv", [D, D], F32, kind="ExternalInput").ap()
    bias_d = nc.dram_tensor("biasr", [1, 3 * D], F32, kind="ExternalInput").ap()
    seg_d = nc.dram_tensor("seg", [128, 16 * 32], F32, kind="ExternalInput").ap()
    idn_d = nc.dram_tensor("idn", [128, 128], F32, kind="ExternalInput").ap()
    own_d = nc.dram_tensor("ownoff", [1, 1], F32, kind="ExternalInput").ap()
    y_d = nc.dram_tensor("y", [D, T], F32, kind="ExternalOutput").ap()

    es = contextlib.ExitStack()
    with es:
        X = sb_at(nc, "X", [128, NCH * T], F32, 0)
        QFM = sb_at(nc, "QFM", [128, NCH * T], BF16, 66 * KB)
        WQ = sb_at(nc, "WQ", [128, NCH * D], BF16, 100 * KB)
        WK = sb_at(nc, "WK", [128, NCH * D], BF16, 116 * KB)
        WV = sb_at(nc, "WV", [128, NCH * D], BF16, 132 * KB)
        wstage = sb_at(nc, "wstage", [128, 2 * D], F32, 148 * KB)
        Ut = sb_at(nc, "Ut", [128, NCH * 512], BF16, 156 * KB)
        QK = sb_at(nc, "QK", [128, D], F32, 164 * KB)
        EX = sb_at(nc, "EX", [128, D], F32, 168 * KB)
        qtok = sb_at(nc, "qtok", [128, D], BF16, 172 * KB)
        ktok = sb_at(nc, "ktok", [128, D], BF16, 174 * KB)
        vtok = sb_at(nc, "vtok", [128, D], BF16, 176 * KB)
        k3t = sb_at(nc, "k3t", [128, D], BF16, 178 * KB)
        kfm = sb_at(nc, "kfm", [128, NCH * 128], BF16, 180 * KB)
        LMK = sb_at(nc, "LMK", [128, NCH * 64], BF16, 182 * KB)
        K2b = sb_at(nc, "K2b", [128, 64], BF16, 183 * KB)
        MIDb = sb_at(nc, "MIDb", [64, NCH * 128], BF16, 183 * KB + 256)
        K1b = sb_at(nc, "K1b", [64, 512], BF16, 183 * KB + 2304)
        segb = sb_at(nc, "segb", [128, 16 * 32], BF16, 183 * KB + 3424)
        idnb = sb_at(nc, "idnb", [128, 128], BF16, 183 * KB + 4512)
        brow = sb_at(nc, "brow", [1, 3 * D], F32, 183 * KB + 4800)
        ps_A = Bank(es.enter_context(nc.psum_tensor("psA", [128, 512], F32)))
        ps_C = Bank(es.enter_context(nc.psum_tensor("psC", [128, 512], F32)))
        pj = [Bank(es.enter_context(nc.psum_tensor(f"pj{i}", [128, 512], F32)))
              for i in range(2)]
        ps_mid = Bank(es.enter_context(nc.psum_tensor("psmid", [64, 1024], F32)))
        ps_lmk = Bank(es.enter_context(nc.psum_tensor("pslmk", [128, 512], F32)))
        ps_tr = Bank(es.enter_context(nc.psum_tensor("pstr", [128, 128], BF16)))
        p = shared_prog(nc, es)
        ctx = make_common(nc, es, p, scr_off=190 * KB)
        sd, sa, sp = ctx["sd"], ctx["sa"], ctx["sp"]
        sl, st = ctx["sl"], ctx["st"]
        ctx["ps_A"], ctx["ps_C"] = ps_A, ps_C

        p.on("sync", lambda e: e.dma_start(out=wstage[:, 0:512], in_=seg_d), inc=sl.inc(16))
        p.on("scalar", lambda e: e.activation(out=segb[:, :], in_=wstage[:, 0:512], func=AF.Identity),
             wait=[sl.tok()], inc=sa.inc())
        p.on("sync", lambda e: e.dma_start(out=wstage[:, 512:640], in_=idn_d), inc=sl.inc(16))
        p.on("scalar", lambda e: e.activation(out=idnb[:, :], in_=wstage[:, 512:640], func=AF.Identity),
             wait=[sl.tok()], inc=sa.inc())
        p.on("sync", lambda e: e.dma_start(out=brow[0:1, :], in_=bias_d), inc=sl.inc(16))
        const2_tok = (sl, sl.v)
        ctx["cast_stage_free"] = [(sa, sa.v)]
        wtoks = {}
        for nm, wd, Wsb in (("q", wq_d, WQ), ("k", wk_d, WK), ("v", wv_d, WV)):
            wtoks[nm] = emit_cast_load(p, ctx, wd, Wsb[:, :], NCH, D, wstage[:, :])

        def bias_proj(tile_u, Wsb, boff, wtok, extra=()):
            for half in range(2):
                bank = pj[half]
                p.on("tensor", lambda e, o=bank.ap[:, :], l=ctx["ones_row"],
                     r=brow[0:1, boff + half * 512: boff + (half + 1) * 512]:
                     e.matmul(o, l, r, start=True, stop=False),
                     wait=[const2_tok, wtok] + bank.free + list(extra), inc=sp.inc())
                for c in range(NCH):
                    p.on("tensor", lambda e, o=bank.ap[:, :],
                         l=tile_u[c], r=Wsb[:, c * D + half * 512: c * D + (half + 1) * 512],
                         lst=(c == NCH - 1):
                         e.matmul(o, l, r, start=False, stop=lst),
                         inc=sp.inc())
            return sp.tok()

        def softmax_tile(out_bf, mm_tok):
            p.on("vector", lambda e: e.tensor_copy(QK[:, 0:512], pj[0].ap[:, :]),
                 wait=[mm_tok], inc=sd.inc())
            p.on("vector", lambda e: e.tensor_copy(QK[:, 512:1024], pj[1].ap[:, :]),
                 inc=sd.inc())
            pj[0].free = [sd.tok()]
            pj[1].free = [sd.tok()]
            mrow = ctx["s_stat"][:, 0:1]
            nrow = ctx["s_stat"][:, 1:2]
            srow = ctx["s_stat"][:, 2:3]
            rrow = ctx["s_stat"][:, 3:4]
            p.on("vector", lambda e: e.tensor_reduce(out=mrow, in_=QK[:, :], axis=AX.X, op=ALU.max),
                 wait=[sd.tok()], inc=sd.inc())
            p.on("vector", lambda e: e.tensor_scalar(out=nrow, in0=mrow, scalar1=-1.0,
                                                     scalar2=None, op0=ALU.mult),
                 wait=[sd.tok()], inc=sd.inc())
            p.on("scalar", lambda e: e.activation(out=EX[:, :], in_=QK[:, :], func=AF.Exp,
                                                  bias=nrow, accum_out=srow),
                 wait=[sd.tok()], inc=sa.inc())
            p.on("vector", lambda e: e.reciprocal(out=rrow, in_=srow),
                 wait=[sa.tok()], inc=sd.inc())
            p.on("vector", lambda e, o=out_bf: e.tensor_scalar(out=o, in0=EX[:, :], scalar1=rrow,
                                                               scalar2=None, op0=ALU.mult),
                 wait=[sd.tok()], inc=sd.inc())
            return sd.tok()

        def transposes(src_bf, dst, dst_off, Tt, ready):
            for c in range(NCH):
                tr_wait = ([ready] if c == 0 else []) + ps_tr.free
                p.on("tensor", lambda e, i=src_bf[:, c * 128:(c + 1) * 128]:
                     e.transpose(ps_tr.ap[:, :], i, idnb[:, :]),
                     wait=tr_wait, inc=sp.inc())
                p.on("vector", lambda e, o=dst[:, c * Tt + dst_off: c * Tt + dst_off + 128]:
                     e.tensor_copy(o, ps_tr.ap[:, :]),
                     wait=[sp.tok()], inc=sd.inc())
                ps_tr.free = [sd.tok()]
            return sd.tok()

        lm_started = [False]

        def pass1(x_src, is_own, lm_off):
            p.on("sync", lambda e: e.dma_start(out=fm3(X[:, :], T), in_=rearr_fm(x_src)),
                 wait=[(sd, sd.v), (sp, sp.v)], inc=sl.inc(16))
            x_tok = (sl, sl.v)
            for g in range(T // 512):
                uready = emit_ln_tile(p, ctx, X[:, :], T, g * 512, 512, Ut[:, :], 0, 512,
                                      extra_wait=[x_tok, ctx["const_tok"]],
                                      u_guard=[(sp, sp.v)])
                for t in range(4 if L1ATTB_STAGE != 10 else 0):
                    tile = g * 4 + t
                    tile_u = [Ut[:, c * 512 + t * 128: c * 512 + t * 128 + 128] for c in range(NCH)]
                    mm = bias_proj(tile_u, WK, D, wtoks["k"], extra=[uready])
                    kready = softmax_tile(ktok[:, :], mm)
                    if L1ATTB_STAGE == 11:
                        continue
                    if L1ATTB_STAGE == 12:
                        continue
                    first_lm = not lm_started[0]
                    lm_started[0] = True
                    for c in range(NCH):
                        p.on("tensor", lambda e, o=ps_lmk.ap[:, c * 32:(c + 1) * 32],
                             l=ktok[:, c * 128:(c + 1) * 128],
                             r=segb[:, tile * 32:(tile + 1) * 32],
                             stt=first_lm:
                             e.matmul(o, l, r, start=stt, stop=False),
                             wait=(([kready] + (list(ps_lmk.free) if first_lm else []))
                                   if c == 0 else []), inc=sp.inc())
                    if is_own and L1ATTB_STAGE != 13:
                        mm = bias_proj(tile_u, WQ, 0, wtoks["q"], extra=[kready])
                        qready = softmax_tile(qtok[:, :], mm)
                        if L1ATTB_STAGE != 14:
                            transposes(qtok[:, :], QFM[:, :], tile * 128, T, qready)
            for c in range(NCH if L1ATTB_STAGE not in (10, 11, 12) else 0):
                p.on("vector", lambda e, o=LMK[:, c * 64 + lm_off: c * 64 + lm_off + 32],
                     i=ps_lmk.ap[:, c * 32:(c + 1) * 32]:
                     e.tensor_copy(o, i), wait=[(sp, sp.v)], inc=sd.inc())
            ps_lmk.free = [(sd, sd.v)]
            lm_started[0] = False
            return (sd, sd.v)

        pass1(x_d, True, L1ATTB_STAGE if False else 0)
        if L1ATTB_STAGE in (0, 2, 3, 4):
            pass1(xo_d, False, 32)

        mid_started = [False]

        def pass2(x_src):
            p.on("sync", lambda e: e.dma_start(out=fm3(X[:, :], T), in_=rearr_fm(x_src)),
                 wait=[(sd, sd.v), (sp, sp.v)], inc=sl.inc(16))
            x_tok = (sl, sl.v)
            for g in range(T // 512):
                uready = emit_ln_tile(p, ctx, X[:, :], T, g * 512, 512, Ut[:, :], 0, 512,
                                      extra_wait=[x_tok, ctx["const_tok"]],
                                      u_guard=[(sp, sp.v)])
                for t in range(4):
                    tile_u = [Ut[:, c * 512 + t * 128: c * 512 + t * 128 + 128] for c in range(NCH)]
                    mm = bias_proj(tile_u, WK, D, wtoks["k"], extra=[uready])
                    kready = softmax_tile(ktok[:, :], mm)
                    mm = bias_proj(tile_u, WV, 2 * D, wtoks["v"], extra=[kready])
                    p.on("scalar", lambda e: e.activation(out=vtok[:, 0:512], in_=pj[0].ap[:, :],
                                                          func=AF.Identity),
                         wait=[mm], inc=sa.inc())
                    p.on("scalar", lambda e: e.activation(out=vtok[:, 512:1024], in_=pj[1].ap[:, :],
                                                          func=AF.Identity),
                         inc=sa.inc())
                    pj[0].free = [sa.tok()]
                    pj[1].free = [sa.tok()]
                    p.on("vector", lambda e: e.tensor_mul(k3t[:, :], ktok[:, :], vtok[:, :]),
                         wait=[sa.tok()], inc=sd.inc())
                    k3ready = sd.tok()
                    kfm_tok = transposes(ktok[:, :], kfm[:, :], 0, 128, k3ready)
                    for c in range(NCH):
                        p.on("tensor", lambda e, o=pj[0].ap[:, 0:64],
                             l=kfm[:, c * 128:(c + 1) * 128], r=LMK[:, c * 64:(c + 1) * 64],
                             stt=(c == 0), lst=(c == NCH - 1):
                             e.matmul(o, l, r, start=stt, stop=lst),
                             wait=([kfm_tok] + pj[0].free) if c == 0 else [], inc=sp.inc())
                    p.on("vector", lambda e: e.tensor_copy(K2b[:, :], pj[0].ap[:, 0:64]),
                         wait=[sp.tok()], inc=sd.inc())
                    pj[0].free = [sd.tok()]
                    first_mid = not mid_started[0]
                    mid_started[0] = True
                    for half in range(2):
                        p.on("tensor", lambda e, o=ps_mid.ap[:, half * 512:(half + 1) * 512],
                             l=K2b[:, :], r=k3t[:, half * 512:(half + 1) * 512],
                             stt=first_mid:
                             e.matmul(o, l, r, start=stt, stop=False),
                             wait=[(sd, sd.v)] if half == 0 else [], inc=sp.inc())

        if L1ATTB_STAGE in (0, 3, 4):
            pass2(xo_d)
        if L1ATTB_STAGE in (0, 4):
            pass2(x_d)

        if L1ATTB_STAGE == 0:
            for c in range(NCH):
                p.on("vector", lambda e, o=MIDb[:, c * 128:(c + 1) * 128],
                     i=ps_mid.ap[:, c * 128:(c + 1) * 128]:
                     e.tensor_copy(o, i), wait=[(sp, sp.v)], inc=sd.inc())
        mid_tok = (sd, sd.v)
        for tt4 in range(T // 512 if L1ATTB_STAGE == 0 else 0):
            t0 = tt4 * 512
            for c in range(NCH):
                p.on("tensor", lambda e, o=pj[0].ap[:, 0:64].tensor[0:64, 0:512] if False else pj[0].ap[0:64, 0:512],
                     l=LMK[:, c * 64:(c + 1) * 64], r=QFM[:, c * T + t0: c * T + t0 + 512],
                     stt=(c == 0), lst=(c == NCH - 1):
                     e.matmul(o, l, r, start=stt, stop=lst),
                     wait=(pj[0].free + [mid_tok]) if c == 0 else [], inc=sp.inc())
            p.on("vector", lambda e: e.tensor_copy(K1b[:, :], pj[0].ap[0:64, 0:512]),
                 wait=[sp.tok()], inc=sd.inc())
            pj[0].free = [sd.tok()]
            k1_tok = sd.tok()
            for c in range(NCH):
                p.on("tensor", lambda e, o=pj[1].ap[:, :],
                     l=MIDb[:, c * 128:(c + 1) * 128], r=K1b[:, :]:
                     e.matmul(o, l, r, start=True, stop=True),
                     wait=[k1_tok] + pj[1].free, inc=sp.inc())
                xc = X[:, c * T + t0: c * T + t0 + 512]
                p.on("vector", lambda e, o=xc, i=pj[1].ap[:, :]:
                     e.scalar_tensor_tensor(out=o, in0=i, scalar=1.0, in1=o,
                                            op0=ALU.mult, op1=ALU.add),
                     wait=[sp.tok()], inc=sd.inc())
                pj[1].free = [sd.tok()]
        fin = (sd, sd.v)
        p.on("sync", lambda e: e.dma_start(out=rearr_fm(y_d), in_=fm3(X[:, :], T)),
             wait=[fin], inc=st.inc(16))
        p.on("gpsimd", lambda e: e.engine_nop(), wait=[st.tok()])
        block = es.enter_context(nc.Block())
        p.replay(block)
    return nc


_L1ATTB_NC = None


def run_l1_attn_b(x_full, Wq, bq, Wk, bk, Wv, bv):
    global _L1ATTB_NC
    if _L1ATTB_NC is None:
        _L1ATTB_NC = build_l1_attn_b()
    biasr = np.concatenate([bq, bk, bv]).reshape(1, 3 * D).astype(np.float32)
    seg = np.zeros((128, 16 * 32), np.float32)
    for tt in range(16):
        seg[0:64, tt * 32 + 2 * tt] = 1.0 / 64
        seg[64:128, tt * 32 + 2 * tt + 1] = 1.0 / 64
    idn = np.eye(128, dtype=np.float32)
    in_maps = []
    for core in range(8):
        b, h = divmod(core, 2)
        xs = np.ascontiguousarray(x_full[b, h * T:(h + 1) * T, :].T)
        xo = np.ascontiguousarray(x_full[b, (1 - h) * T:(2 - h) * T, :].T)
        in_maps.append({"x": xs, "xo": xo, "wq": np.ascontiguousarray(Wq),
                        "wk": np.ascontiguousarray(Wk), "wv": np.ascontiguousarray(Wv),
                        "biasr": biasr, "seg": seg, "idn": idn,
                        "ownoff": np.zeros((1, 1), np.float32)})
    res = run_bass_kernel_spmd(_L1ATTB_NC, in_maps, core_ids=list(range(8)))
    out = np.empty((B, S, D), np.float32)
    for core in range(8):
        b, h = divmod(core, 2)
        out[b, h * T:(h + 1) * T, :] = res.results[core]["y"].T
    return out





def emit_cast_spill(p, ctx, w_dram, wbf_dram, R, C, stage, bfslots):
    sl, sa, st = ctx["sl"], ctx["sa"], ctx["st"]
    wr = w_dram.rearrange("(c p) m -> c p m", p=128)
    wbr = wbf_dram.rearrange("(c p) m -> c p m", p=128)
    toks = []
    for k in range(R // 128):
        buf = stage[:, (k % 2) * C:(k % 2) * C + C]
        slot = bfslots[:, (k % 2) * C:(k % 2) * C + C]
        guard = [toks[k - 2][0]] if k >= 2 else list(ctx.get("cast_stage_free", []))
        p.on("sync", lambda e, o=buf, i=wr[k]: e.dma_start(out=o, in_=i),
             wait=guard, inc=sl.inc(16))
        cg = [toks[k - 2][1]] if k >= 2 else []
        p.on("scalar", lambda e, o=slot, i=buf: e.activation(out=o, in_=i, func=AF.Identity),
             wait=[sl.tok()] + cg, inc=sa.inc())
        p.on("sync", lambda e, o=wbr[k], i=slot: e.dma_start(out=o, in_=i),
             wait=[sa.tok()], inc=st.inc(16))
        toks.append((sa.tok(), st.tok()))
    ctx["cast_stage_free"] = [toks[-1][0], toks[-2][0]]
    return st.tok()


L0A_STAGE = 0
L0A_SUB = 0
def build_l0_attn_a():
    nc = new_core("l0atta" + str(L0A_STAGE))
    x_d = nc.dram_tensor("x", [D, T], F32, kind="ExternalInput").ap()
    xo_d = nc.dram_tensor("xo", [D, T], F32, kind="ExternalInput").ap()
    wq_d = nc.dram_tensor("wq", [D, D], F32, kind="ExternalInput").ap()
    wk_d = nc.dram_tensor("wk", [D, D], F32, kind="ExternalInput").ap()
    wv_d = nc.dram_tensor("wv", [D, D], F32, kind="ExternalInput").ap()
    wo_d = nc.dram_tensor("wo", [D, D], F32, kind="ExternalInput").ap()
    g1_d = nc.dram_tensor("g1", [128, 8], F32, kind="ExternalInput").ap()
    b1_d = nc.dram_tensor("b1", [128, 8], F32, kind="ExternalInput").ap()
    g2_d = nc.dram_tensor("g2", [128, 8], F32, kind="ExternalInput").ap()
    b2_d = nc.dram_tensor("b2", [128, 8], F32, kind="ExternalInput").ap()
    cw1_d = nc.dram_tensor("cw1", [3 * D, 4 * D], F32, kind="ExternalInput").ap()
    cw2_d = nc.dram_tensor("cw2", [3 * 4 * D, D], F32, kind="ExternalInput").ap()
    cb1_d = nc.dram_tensor("cb1", [128, 3 * 32], F32, kind="ExternalInput").ap()
    cb2_d = nc.dram_tensor("cb2", [128, 3 * 8], F32, kind="ExternalInput").ap()
    mask_d = nc.dram_tensor("maskt", [128, 128], F32, kind="ExternalInput").ap()
    vflag_d = nc.dram_tensor("vflag", [128, 1], F32, kind="ExternalInput").ap()
    y_d = nc.dram_tensor("y", [D, T], F32, kind="ExternalOutput").ap()
    u2_d = nc.dram_tensor("u2spill", [D, T], F32, kind="Internal").ap()
    w2bf_d = nc.dram_tensor("w2bf", [4 * D, D], BF16, kind="Internal").ap()

    es = contextlib.ExitStack()
    with es:
        U2 = sb_at(nc, "U2", [128, NCH * T], F32, 0)
        WA = sb_at(nc, "WA", [128, 4 * NCH * D], BF16, 64 * KB)
        W1c = sb_at(nc, "W1c", [128, NCH * 4 * D], BF16, 64 * KB)
        rows = sb_at(nc, "rows", [128, 2 * T], F32, 128 * KB)
        XSTG = sb_at(nc, "XSTG", [128, NCH * 512], F32, 144 * KB)
        U1C = sb_at(nc, "U1C", [128, NCH * 128], BF16, 160 * KB)
        QF = sb_at(nc, "QF", [128, NCH * 128], BF16, 162 * KB)
        KF = sb_at(nc, "KF", [128, NCH * 128], BF16, 164 * KB)
        KT = sb_at(nc, "KT", [128, D], BF16, 166 * KB)
        VT = sb_at(nc, "VT", [128, D], BF16, 168 * KB)
        YT = sb_at(nc, "YT", [128, NCH * 128], BF16, 170 * KB)
        Sm = sb_at(nc, "Sm", [128, 256], BF16, 172 * KB)
        M_sb = sb_at(nc, "M_sb", [128, NCH * 64], F32, 173 * KB)
        M_bf = sb_at(nc, "M_bf", [128, 64], BF16, 175 * KB)
        maskb = sb_at(nc, "maskb", [128, 128], BF16, 176 * KB)
        vflag = sb_at(nc, "vflag_sb", [128, 1], F32, 176 * KB + 256)
        g1_sb = sb_at(nc, "g1_sb", [128, 8], F32, 176 * KB + 288)
        b1_sb = sb_at(nc, "b1_sb", [128, 8], F32, 176 * KB + 320)
        g2_sb = sb_at(nc, "g2_sb", [128, 8], F32, 176 * KB + 352)
        b2_sb = sb_at(nc, "b2_sb", [128, 8], F32, 176 * KB + 384)
        cb1_sb = sb_at(nc, "cb1_sb", [128, 3 * 32], F32, 176 * KB + 416)
        cb2_sb = sb_at(nc, "cb2_sb", [128, 3 * 8], F32, 176 * KB + 800)
        wstage = sb_at(nc, "wstage", [128, 2 * D], F32, 177 * KB)
        onesP = sb_at(nc, "onesP", [128, 128], F32, 185 * KB)
        HID = sb_at(nc, "HID", [128, 32 * 512], BF16, 0)
        W2s = sb_at(nc, "W2s", [128, 2 * 32 * 128], BF16, 32 * KB)
        cstage = sb_at(nc, "cstage", [128, 2 * 2048], F32, 48 * KB)
        Hc = sb_at(nc, "Hc", [128, NCH * T], BF16, 128 * KB)
        cbf = sb_at(nc, "cbf", [128, 2 * 2048], BF16, 160 * KB)
        FU2 = sb_at(nc, "FU2", [128, NCH * 256], F32, 160 * KB)
        FX = sb_at(nc, "FX", [128, NCH * 256], F32, 168 * KB)
        FY = sb_at(nc, "FY", [128, NCH * 256], F32, 176 * KB)
        ps = [Bank(es.enter_context(nc.psum_tensor(f"ps{i}", [128, 512], F32)))
              for i in range(8)]
        p = shared_prog(nc, es)
        ctx = make_common(nc, es, p, scr_off=190 * KB)
        sd, sa, sp = ctx["sd"], ctx["sa"], ctx["sp"]
        sl, st, sw = ctx["sl"], ctx["st"], ctx["sw"]
        ctx["ps_A"], ctx["ps_C"] = ps[0], ps[1]
        b0, b1k = ps[0], ps[1]
        pbank = (ps[2], ps[3])
        sbank = (ps[4], ps[5])
        obank = (ps[6], ps[7])

        WQ, WK, WV, WO = (WA[:, i * NCH * D:(i + 1) * NCH * D] for i in range(4))

        for (dst, srcd) in ((g1_sb, g1_d), (b1_sb, b1_d), (g2_sb, g2_d), (b2_sb, b2_d),
                            (cb1_sb, cb1_d), (cb2_sb, cb2_d), (vflag, vflag_d)):
            p.on("sync", lambda e, o=dst[:, :], i=srcd: e.dma_start(out=o, in_=i), inc=sl.inc(16))
        p.on("sync", lambda e: e.dma_start(out=XSTG[:, 0:128], in_=mask_d), inc=sl.inc(16))
        p.on("scalar", lambda e: e.activation(out=maskb[:, :], in_=XSTG[:, 0:128], func=AF.Identity),
             wait=[sl.tok()], inc=sa.inc())
        cn_tok = (sl, sl.v)
        mcast_tok = (sa, sa.v)
        ctx["cast_stage_free"] = [mcast_tok]
        wq_tok = emit_cast_load(p, ctx, wq_d, WQ, NCH, D, wstage[:, :])
        wk_tok = emit_cast_load(p, ctx, wk_d, WK, NCH, D, wstage[:, :])
        wv_tok = emit_cast_load(p, ctx, wv_d, WV, NCH, D, wstage[:, :])
        wo_tok = emit_cast_load(p, ctx, wo_d, WO, NCH, D, wstage[:, :])
        w_tok = wo_tok
        p.on("gpsimd", lambda e: e.memset(M_sb[:, :], 0.0), inc=ctx["sg"].inc())
        p.on("gpsimd", lambda e: e.memset(onesP[:, :], 1.0), inc=ctx["sg"].inc())
        m0_tok = (ctx["sg"], ctx["sg"].v)

        xr = rearr_fm(x_d)
        xor_ = rearr_fm(xo_d)

        def ln_rows_pass(src_re):
            last = None
            for tt in range(T // 512):
                t0 = tt * 512
                g = [last] if last else [cn_tok]
                p.on("sync", lambda e, o=fm3(XSTG[:, :], 512), i=src_re[:, :, t0:t0 + 512]:
                     e.dma_start(out=o, in_=i), wait=g + [(sd, sd.v), (sa, sa.v)],
                     inc=sl.inc(16))
                x_tok = (sl, sl.v)
                last = emit_ln_rows(p, ctx, XSTG[:, :], 512, 0, 512, rows[:, :], T,
                                    extra_wait=[x_tok, ctx["const_tok"]], ln1=True, rt0=t0)
            return last

        def emit_chunk(src_re, c, own):
            t0 = c * 128
            slot = (c % 2) * NCH * 128
            xa = XSTG[:, slot:slot + NCH * 128]
            xa3 = [xa[:, dc * 128:(dc + 1) * 128] for dc in range(NCH)]
            p.on("sync", lambda e, o=fm3(xa, 128), i=src_re[:, :, t0:t0 + 128]:
                 e.dma_start(out=o, in_=i), wait=[(sd, sd.v)], inc=sl.inc(16))
            x_tok = (sl, sl.v)
            p.on("tensor", lambda e, r=rows[64:65, t0:t0 + 128]:
                 e.matmul(b0.ap[:, 0:128], onesP[64:65, 0:128], r, start=True, stop=True),
                 wait=b0.free + [m0_tok], inc=sp.inc())
            p.on("tensor", lambda e, r=rows[0:1, T + t0: T + t0 + 128]:
                 e.matmul(b1k.ap[:, 0:128], ctx["ones_row"], r, start=True, stop=True),
                 wait=b1k.free, inc=sp.inc())
            bc_tok = sp.tok()
            for dc in range(NCH):
                tm = ctx["s_f32"][:, (dc % 2) * 512:(dc % 2) * 512 + 128]
                p.on("vector", lambda e, o=tm, a=xa3[dc]: e.tensor_mul(o, a, b0.ap[:, 0:128]),
                     wait=[bc_tok, x_tok] + ([(sd, sd.v)] if dc >= 2 else []), inc=sd.inc())
                p.on("vector", lambda e, o=tm, a=tm: e.tensor_sub(o, a, b1k.ap[:, 0:128]),
                     wait=[sd.tok()], inc=sd.inc())
                p.on("vector", lambda e, o=U1C[:, dc * 128:(dc + 1) * 128], a=tm, d_=dc:
                     e.tensor_scalar(out=o, in0=a, scalar1=g1_sb[:, d_:d_ + 1],
                                     scalar2=b1_sb[:, d_:d_ + 1], op0=ALU.mult, op1=ALU.add),
                     wait=[sd.tok()], inc=sd.inc())
            u1_tok = sd.tok()
            if not own:
                b0.free = [u1_tok]
                b1k.free = [u1_tok]
            def proj_fm(Wsb, dst, extra):
                for dc in range(NCH):
                    bank = pbank[dc % 2]
                    for k in range(NCH):
                        p.on("tensor", lambda e, o=bank.ap[:, 0:128],
                             l=Wsb[:, k * D + dc * 128: k * D + (dc + 1) * 128],
                             r=U1C[:, k * 128:(k + 1) * 128],
                             s_=(k == 0), e_=(k == NCH - 1):
                             e.matmul(o, l, r, start=s_, stop=e_),
                             wait=([u1_tok] + bank.free + list(extra)) if k == 0 else [],
                             inc=sp.inc())
                    p.on("vector", lambda e, o=dst[:, dc * 128:(dc + 1) * 128], i=bank.ap[:, 0:128]:
                         e.tensor_copy(o, i), wait=[sp.tok()], inc=sd.inc())
                    bank.free = [sd.tok()]

            def proj_tok(Wsb, dst, extra, vscale=None):
                for hf in range(2):
                    bank = pbank[hf % 2]
                    for k in range(NCH):
                        p.on("tensor", lambda e, o=bank.ap[:, :],
                             l=U1C[:, k * 128:(k + 1) * 128],
                             r=Wsb[:, k * D + hf * 512: k * D + (hf + 1) * 512],
                             s_=(k == 0), e_=(k == NCH - 1):
                             e.matmul(o, l, r, start=s_, stop=e_),
                             wait=([u1_tok] + bank.free + list(extra)) if k == 0 else [],
                             inc=sp.inc())
                    if vscale is None:
                        p.on("vector", lambda e, o=dst[:, hf * 512:(hf + 1) * 512], i=bank.ap[:, :]:
                             e.tensor_copy(o, i), wait=[sp.tok()], inc=sd.inc())
                        bank.free = [sd.tok()]
                    else:
                        p.on("scalar", lambda e, o=dst[:, hf * 512:(hf + 1) * 512], i=bank.ap[:, :]:
                             e.activation(out=o, in_=i, func=AF.Identity, scale=vscale),
                             wait=[sp.tok()], inc=sa.inc())
                        bank.free = [sa.tok()]

            if own:
                proj_fm(WQ, QF, [w_tok])
                proj_fm(WK, KF, [])
            proj_tok(WK, KT, [w_tok] if not own else [])
            proj_tok(WV, VT, [], vscale=(1.0 if own else vflag[:, 0:1]))
            kv_tok = (sd, sd.v)
            va_tok = (sa, sa.v)
            if own and L0A_SUB == 1:
                return
            use_scores = L0A_SUB in (0, 3, 4)
            use_intra = L0A_SUB in (0, 4)

            mupd = []
            for hp in range(NCH):
                ob = obank[hp % 2]
                sb_ = sbank[hp % 2]
                mdb = pbank[hp % 2]
                p.on("scalar", lambda e, o=M_bf[:, :], i=M_sb[:, hp * 64:(hp + 1) * 64]:
                     e.activation(out=o, in_=i, func=AF.Identity),
                     wait=[m0_tok, (sd, sd.v), (sa, sa.v)], inc=sa.inc())
                mc_tok = sa.tok()
                if own and use_intra:
                    p.on("tensor", lambda e, oo=ob.ap[0:64, 0:128], l=M_bf[0:64, :],
                         r=QF[0:64, hp * 128:(hp + 1) * 128]:
                         e.matmul(oo, l, r, start=True, stop=False, tile_position=(0, 0)),
                         wait=[mc_tok] + ob.free, inc=sp.inc())
                    p.on("tensor", lambda e, oo=ob.ap[64:128, 0:128], l=M_bf[64:128, :],
                         r=QF[64:128, hp * 128:(hp + 1) * 128]:
                         e.matmul(oo, l, r, start=True, stop=False, tile_position=(64, 64)),
                         inc=sp.inc())
                if own and use_scores:
                    p.on("tensor", lambda e, oo=sbank[0].ap[:, 0:128],
                         l=KF[0:64, hp * 128:(hp + 1) * 128], r=QF[0:64, hp * 128:(hp + 1) * 128]:
                         e.matmul(oo, l, r, start=True, stop=True, tile_position=(0, 0)),
                         wait=sbank[0].free, inc=sp.inc())
                    p.on("tensor", lambda e, oo=sbank[1].ap[:, 0:128],
                         l=KF[64:128, hp * 128:(hp + 1) * 128], r=QF[64:128, hp * 128:(hp + 1) * 128]:
                         e.matmul(oo, l, r, start=True, stop=True, tile_position=(64, 0)),
                         wait=sbank[1].free, inc=sp.inc())
                    sc_tok = sp.tok()
                    p.on("vector", lambda e, oo=Sm[:, 0:128], a=sbank[0].ap[:, 0:128]:
                         e.tensor_mul(oo, a, maskb[:, :]), wait=[sc_tok], inc=sd.inc())
                    sbank[0].free = [sd.tok()]
                    p.on("vector", lambda e, oo=Sm[:, 128:256], a=sbank[1].ap[:, 0:128]:
                         e.tensor_mul(oo, a, maskb[:, :]), inc=sd.inc())
                    sbank[1].free = [sd.tok()]
                    sm_tok = sd.tok()
                if own and use_intra and use_scores:
                    p.on("tensor", lambda e, oo=ob.ap[0:64, 0:128],
                         l=VT[:, (2 * hp) * 64:(2 * hp + 1) * 64], r=Sm[:, 0:128]:
                         e.matmul(oo, l, r, start=False, stop=False, tile_position=(0, 0)),
                         wait=[sm_tok, va_tok], inc=sp.inc())
                    p.on("tensor", lambda e, oo=ob.ap[64:128, 0:128],
                         l=VT[:, (2 * hp + 1) * 64:(2 * hp + 2) * 64], r=Sm[:, 128:256]:
                         e.matmul(oo, l, r, start=False, stop=True, tile_position=(0, 64)),
                         inc=sp.inc())
                p.on("tensor", lambda e, oo=mdb.ap[0:64, 0:64],
                     l=KT[:, (2 * hp) * 64:(2 * hp + 1) * 64],
                     r=VT[:, (2 * hp) * 64:(2 * hp + 1) * 64]:
                     e.matmul(oo, l, r, start=True, stop=False, tile_position=(0, 0)),
                     wait=[kv_tok, va_tok] + mdb.free, inc=sp.inc())
                p.on("tensor", lambda e, oo=mdb.ap[64:128, 0:64],
                     l=KT[:, (2 * hp + 1) * 64:(2 * hp + 2) * 64],
                     r=VT[:, (2 * hp + 1) * 64:(2 * hp + 2) * 64]:
                     e.matmul(oo, l, r, start=True, stop=True, tile_position=(0, 64)),
                     inc=sp.inc())
                md_tok = sp.tok()
                if own and use_intra:
                    p.on("vector", lambda e, o=YT[:, hp * 128:(hp + 1) * 128], i=ob.ap[:, 0:128]:
                         e.tensor_copy(o, i), wait=[sp.tok()], inc=sd.inc())
                    ob.free = [sd.tok()]
                p.on("vector", lambda e, o=M_sb[:, hp * 64:(hp + 1) * 64], i=mdb.ap[:, 0:64]:
                     e.tensor_add(o, o, i), wait=[md_tok], inc=sd.inc())
                mdb.free = [sd.tok()]
                mupd.append(sd.tok())
            if not own or not (use_intra and use_scores):
                return
            yt_tok = (sd, sd.v)
            p.on("tensor", lambda e, r=rows[0:1, t0:t0 + 128]:
                 e.matmul(b0.ap[:, 128:256], ctx["ones_row"], r, start=True, stop=True),
                 wait=[u1_tok], inc=sp.inc())
            p.on("tensor", lambda e, r=rows[32:33, t0:t0 + 128]:
                 e.matmul(b1k.ap[:, 128:256], onesP[32:33, 0:128], r, start=True, stop=True),
                 inc=sp.inc())
            bc2_tok = sp.tok()
            for dc in range(NCH):
                bank = pbank[dc % 2]
                for k in range(NCH):
                    p.on("tensor", lambda e, o=bank.ap[:, 0:128],
                         l=WO[:, k * D + dc * 128: k * D + (dc + 1) * 128],
                         r=YT[:, k * 128:(k + 1) * 128],
                         s_=(k == 0), e_=(k == NCH - 1):
                         e.matmul(o, l, r, start=s_, stop=e_),
                         wait=([yt_tok] + bank.free) if k == 0 else [], inc=sp.inc())
                tm = ctx["s_f32"][:, (dc % 2) * 512:(dc % 2) * 512 + 128]
                p.on("vector", lambda e, o=tm, a=xa3[dc]: e.tensor_mul(o, a, b0.ap[:, 128:256]),
                     wait=[bc2_tok, (sd, sd.v)], inc=sd.inc())
                p.on("vector", lambda e, o=tm, a=tm: e.tensor_sub(o, a, b1k.ap[:, 128:256]),
                     wait=[sd.tok()], inc=sd.inc())
                p.on("vector", lambda e, o=U2[:, dc * T + t0: dc * T + t0 + 128], a=tm, bk=bank:
                     e.tensor_add(o, a, bk.ap[:, 0:128]),
                     wait=[sp.tok()], inc=sd.inc())
                bank.free = [sd.tok()]
            b0.free = [sd.tok()]
            b1k.free = [sd.tok()]

        ln_rows_pass(xor_)
        for c in range(T // 128 if L0A_STAGE != 1 else 0):
            emit_chunk(xor_, c, own=False)
        ln_rows_pass(xr)
        for c in range(T // 128 if L0A_STAGE not in (1, 2) else 0):
            emit_chunk(xr, c, own=True)
        scan_tok_d = (sd, sd.v)
        scan_tok_p = (sp, sp.v)
        if L0A_STAGE in (1, 2, 3):
            p.on("sync", lambda e: e.dma_start(out=rearr_fm(y_d), in_=fm3(U2[:, :], T)),
                 wait=[scan_tok_d], inc=st.inc(16))
            p.on("gpsimd", lambda e: e.engine_nop(), wait=[st.tok()])
            block = es.enter_context(nc.Block())
            p.replay(block)
            return nc

        for tt in range(T // 512):
            ln2_tok = emit_ln_tile(p, ctx, U2[:, :], T, tt * 512, 512, Hc[:, :], tt * 512, T,
                                   extra_wait=[scan_tok_d], u_guard=[],
                                   gb=(g2_sb, b2_sb))
        p.on("sync", lambda e: e.dma_start(out=rearr_fm(u2_d), in_=fm3(U2[:, :], T)),
             wait=[ln2_tok], inc=st.inc(16))
        spill_tok = (st, st.v)

        if L0A_STAGE == 4:
            p.on("sync", lambda e: e.dma_start(out=rearr_fm(y_d), in_=fm3(U2[:, :], T)),
                 wait=[spill_tok, (sd, sd.v)], inc=st.inc(16))
            p.on("gpsimd", lambda e: e.engine_nop(), wait=[st.tok()])
            block = es.enter_context(nc.Block())
            p.replay(block)
            return nc
        for l in range(3):
            ctx["cast_stage_free"] = [spill_tok, (sp, sp.v), (sa, sa.v)]
            w2spill = emit_cast_spill(p, ctx, cw2_d[l * 4 * D:(l + 1) * 4 * D, :], w2bf_d,
                                      4 * D, D, cstage[:, :], cbf[:, :])
            w1toks = []
            for hf in range(2):
                w1toks.append(emit_cast_load(
                    p, ctx, cw1_d[l * D:(l + 1) * D, hf * 2 * D:(hf + 1) * 2 * D],
                    W1c, NCH, 2 * D, cstage[:, :], dstride=4 * D, doff=hf * 2 * D))
            w1_tok = w1toks[-1]
            w2r = w2bf_d
            nw2 = 0
            for tt in range(T // 512):
                t0 = tt * 512
                for m in range(32):
                    bank = pbank[m % 2]
                    for k in range(NCH):
                        p.on("tensor", lambda e, o=bank.ap[:, :],
                             l=W1c[:, k * 4 * D + m * 128: k * 4 * D + (m + 1) * 128],
                             r=Hc[:, k * T + t0: k * T + t0 + 512],
                             s_=(k == 0), e_=(k == NCH - 1):
                             e.matmul(o, l, r, start=s_, stop=e_),
                             wait=([w1_tok, spill_tok] + bank.free) if k == 0 else [],
                             inc=sp.inc())
                    p.on("scalar", lambda e, o=HID[:, m * 512:(m + 1) * 512], i=bank.ap[:, :],
                         bia=cb1_sb[:, l * 32 + m:l * 32 + m + 1]:
                         e.activation(out=o, in_=i, func=AF.Gelu_apprx_tanh, bias=bia),
                         wait=[sp.tok()], inc=sa.inc())
                    bank.free = [sa.tok()]
                h_tok = sa.tok()
                for dc in range(NCH):
                    slot2 = nw2 % 2
                    w2slice = w2r[:, dc * 128:(dc + 1) * 128].rearrange("(m p) q -> p m q", p=128)
                    dst = W2s[:, slot2 * 4096:(slot2 + 1) * 4096].rearrange("p (m q) -> p m q", q=128)
                    p.on("scalar", lambda e, o=dst, i=w2slice: e.dma_start(out=o, in_=i),
                         wait=[(sp, sp.v)] if nw2 >= 2 else [w2spill], inc=sw.inc(16))
                    w2_tok = (sw, sw.v)
                    nw2 += 1
                    bank = pbank[dc % 2]
                    for m in range(32):
                        p.on("tensor", lambda e, o=bank.ap[:, :],
                             l=W2s[:, slot2 * 4096 + m * 128: slot2 * 4096 + (m + 1) * 128],
                             r=HID[:, m * 512:(m + 1) * 512],
                             s_=(m == 0), e_=(m == 31):
                             e.matmul(o, l, r, start=s_, stop=e_),
                             wait=([h_tok, w2_tok] + bank.free) if m == 0 else [],
                             inc=sp.inc())
                    p.on("scalar", lambda e, o=Hc[:, dc * T + t0: dc * T + t0 + 512],
                         i=bank.ap[:, :], bia=cb2_sb[:, l * 8 + dc:l * 8 + dc + 1]:
                         e.activation(out=o, in_=i, func=AF.Identity, bias=bia),
                         wait=[sp.tok(), (sa, sa.v)], inc=sa.inc())
                    bank.free = [sa.tok()]
        casc_tok = (sa, sa.v)

        u2r = rearr_fm(u2_d)
        yr = rearr_fm(y_d)
        last_store = casc_tok
        for tt in range(T // 256):
            t0 = tt * 256
            g = [last_store, (st, st.v)]
            p.on("sync", lambda e, o=fm3(FU2[:, :], 256), i=u2r[:, :, t0:t0 + 256]:
                 e.dma_start(out=o, in_=i), wait=[spill_tok] + g, inc=sl.inc(16))
            p.on("sync", lambda e, o=fm3(FX[:, :], 256), i=xr[:, :, t0:t0 + 256]:
                 e.dma_start(out=o, in_=i), inc=sl.inc(16))
            f_tok = (sl, sl.v)
            for dc in range(NCH):
                p.on("vector", lambda e, o=FY[:, dc * 256:(dc + 1) * 256],
                     a=FU2[:, dc * 256:(dc + 1) * 256],
                     b=Hc[:, dc * T + t0: dc * T + t0 + 256]:
                     e.tensor_add(o, a, b),
                     wait=[f_tok, casc_tok] if dc == 0 else [], inc=sd.inc())
                p.on("vector", lambda e, o=FY[:, dc * 256:(dc + 1) * 256],
                     a=FX[:, dc * 256:(dc + 1) * 256]:
                     e.tensor_add(o, o, a), wait=[sd.tok()], inc=sd.inc())
            p.on("sync", lambda e, o=yr[:, :, t0:t0 + 256], i=fm3(FY[:, :], 256):
                 e.dma_start(out=o, in_=i), wait=[(sd, sd.v)], inc=st.inc(16))
            last_store = (st, st.v)
        p.on("gpsimd", lambda e: e.engine_nop(), wait=[(st, st.v)])
        block = es.enter_context(nc.Block())
        p.replay(block)
    return nc


_L0ATTA_NC = None


def run_l0_attn_a(x_full, ln1_g, ln1_b, wq, wk, wv, wo, ln2_g, ln2_b,
                  cms_w1, cms_b1, cms_w2, cms_b2):
    global _L0ATTA_NC
    if _L0ATTA_NC is None:
        _L0ATTA_NC = build_l0_attn_a()
    r8 = lambda v: np.ascontiguousarray(v.reshape(8, 128).T)
    g1 = r8(ln1_g); b1 = r8(ln1_b); g2 = r8(ln2_g); b2 = r8(ln2_b)
    cw1 = np.ascontiguousarray(cms_w1.reshape(3 * D, 4 * D))
    cw2 = np.ascontiguousarray(cms_w2.reshape(3 * 4 * D, D))
    cb1 = np.ascontiguousarray(np.concatenate(
        [cms_b1[l].reshape(32, 128).T for l in range(3)], axis=1))
    cb2 = np.ascontiguousarray(np.concatenate(
        [cms_b2[l].reshape(8, 128).T for l in range(3)], axis=1))
    maskt = np.triu(np.ones((128, 128), np.float32))
    in_maps = []
    for core in range(8):
        b, h = divmod(core, 2)
        xs = np.ascontiguousarray(x_full[b, h * T:(h + 1) * T, :].T)
        if h == 0:
            xo = np.zeros((D, T), np.float32)
            vf = np.zeros((128, 1), np.float32)
        else:
            xo = np.ascontiguousarray(x_full[b, 0:T, :].T)
            vf = np.ones((128, 1), np.float32)
        in_maps.append({"x": xs, "xo": xo, "wq": np.ascontiguousarray(wq),
                        "wk": np.ascontiguousarray(wk), "wv": np.ascontiguousarray(wv),
                        "wo": np.ascontiguousarray(wo),
                        "g1": g1, "b1": b1, "g2": g2, "b2": b2,
                        "cw1": cw1, "cw2": cw2, "cb1": cb1, "cb2": cb2,
                        "maskt": maskt, "vflag": vf})
    res = run_bass_kernel_spmd(_L0ATTA_NC, in_maps, core_ids=list(range(8)))
    out = np.empty((B, S, D), np.float32)
    for core in range(8):
        b, h = divmod(core, 2)
        out[b, h * T:(h + 1) * T, :] = res.results[core]["y"].T
    return out


def build_l0_attn_b():
    nc = new_core("l0attb")
    x_d = nc.dram_tensor("x", [D, T], F32, kind="ExternalInput").ap()
    xo_d = nc.dram_tensor("xo", [D, T], F32, kind="ExternalInput").ap()
    wq_d = nc.dram_tensor("wq", [D, D], F32, kind="ExternalInput").ap()
    wkT_d = nc.dram_tensor("wkT", [D, D], F32, kind="ExternalInput").ap()
    wvT_d = nc.dram_tensor("wvT", [D, D], F32, kind="ExternalInput").ap()
    wdn_d = nc.dram_tensor("wdn", [D, 256], F32, kind="ExternalInput").ap()
    wuk_d = nc.dram_tensor("wuk", [256, D], F32, kind="ExternalInput").ap()
    wuv_d = nc.dram_tensor("wuv", [256, D], F32, kind="ExternalInput").ap()
    wo_d = nc.dram_tensor("wo", [D, D], F32, kind="ExternalInput").ap()
    dpart_d = nc.dram_tensor("dpart", [128, 64], F32, kind="ExternalInput").ap()
    bcrows_d = nc.dram_tensor("bcrows", [3, T], F32, kind="ExternalInput").ap()
    d2_d = nc.dram_tensor("d2t", [128, 4 * 512], F32, kind="ExternalInput").ap()
    y_d = nc.dram_tensor("y", [D, T], F32, kind="ExternalOutput").ap()

    es = contextlib.ExitStack()
    with es:
        U = sb_at(nc, "U", [128, NCH * T], BF16, 0)
        LAT = sb_at(nc, "LAT", [128, 2 * 4096], BF16, 32 * KB)
        YF = sb_at(nc, "YF", [128, NCH * T], BF16, 48 * KB)
        WQb = sb_at(nc, "WQb", [128, NCH * D], BF16, 80 * KB)
        WOb = sb_at(nc, "WOb", [128, NCH * D], BF16, 96 * KB)
        WKVD = sb_at(nc, "WKVD", [128, NCH * 256], BF16, 112 * KB)
        WUK = sb_at(nc, "WUK", [128, 2 * D], BF16, 116 * KB)
        WUV = sb_at(nc, "WUV", [128, 2 * D], BF16, 120 * KB)
        WDNb = sb_at(nc, "WDNb", [128, NCH * 256], BF16, 124 * KB)
        BCm = sb_at(nc, "BCm", [128, T], F32, 124 * KB)
        BCp = sb_at(nc, "BCp", [128, T], F32, 132 * KB)
        BCr = sb_at(nc, "BCr", [128, T], F32, 140 * KB)
        QH = sb_at(nc, "QH", [128, T], BF16, 148 * KB)
        Qm = sb_at(nc, "Qm", [128, T], BF16, 152 * KB)
        Qp = sb_at(nc, "Qp", [128, T], BF16, 156 * KB)
        Qr = sb_at(nc, "Qr", [128, T], BF16, 160 * KB)
        XSTG = sb_at(nc, "XSTG", [128, NCH * 512], F32, 148 * KB)
        UTMP = sb_at(nc, "UTMP", [128, NCH * 512], BF16, 164 * KB)
        KR = sb_at(nc, "KR", [128, 4096], BF16, 164 * KB)
        VR = sb_at(nc, "VR", [128, 4096], BF16, 172 * KB)
        FX = sb_at(nc, "FX", [128, NCH * 512], F32, 148 * KB)
        YS = sb_at(nc, "YS", [128, NCH * 512], F32, 164 * KB)
        SX = sb_at(nc, "SX", [128, 2 * 1024], BF16, 180 * KB)
        SF = sb_at(nc, "SF", [128, 1024], BF16, 184 * KB)
        Osb = sb_at(nc, "Osb", [128, 512], BF16, 186 * KB)
        D2b = sb_at(nc, "D2b", [128, 4 * 512], BF16, 96 * KB)
        dpart = sb_at(nc, "dpart_sb", [128, 64], F32, 100 * KB)
        onesP = sb_at(nc, "onesP", [128, 128], F32, 100 * KB + 512)
        rrow = sb_at(nc, "rrow", [1, 2048], F32, 101 * KB)
        ps_A = Bank(es.enter_context(nc.psum_tensor("psA", [128, 512], F32)))
        ps_C = Bank(es.enter_context(nc.psum_tensor("psC", [128, 512], F32)))
        S0 = Bank(es.enter_context(nc.psum_tensor("S0", [128, 1024], F32)))
        S1 = Bank(es.enter_context(nc.psum_tensor("S1", [128, 1024], F32)))
        psO = Bank(es.enter_context(nc.psum_tensor("psO", [128, 512], F32)))
        psden = Bank(es.enter_context(nc.psum_tensor("psden", [128, 512], F32)))
        p = shared_prog(nc, es)
        ctx = make_common(nc, es, p, scr_off=187 * KB)
        sd, sa, sp = ctx["sd"], ctx["sa"], ctx["sp"]
        sl, st, sw = ctx["sl"], ctx["st"], ctx["sw"]
        ctx["ps_A"], ctx["ps_C"] = ps_A, ps_C
        pbank = (psO, psden)

        p.on("sync", lambda e: e.dma_start(out=dpart[:, :], in_=dpart_d), inc=sl.inc(16))
        p.on("sync", lambda e: e.dma_start(out=XSTG[:, 0:2048], in_=d2_d), inc=sl.inc(16))
        p.on("scalar", lambda e: e.activation(out=D2b[:, :], in_=XSTG[:, 0:2048], func=AF.Identity),
             wait=[sl.tok()], inc=sa.inc())
        p.on("gpsimd", lambda e: e.memset(onesP[:, :], 1.0), inc=ctx["sg"].inc())
        cn_tok = (sl, sl.v)
        ctx["cast_stage_free"] = [(sa, sa.v)]
        wq_tok = emit_cast_load(p, ctx, wq_d, WQb, NCH, D, XSTG[:, 0:2048])
        wdn_tok = emit_cast_load(p, ctx, wdn_d, WDNb, NCH, 256, XSTG[:, 0:512])
        wuk_tok = emit_cast_load(p, ctx, wuk_d, WUK, 2, D, XSTG[:, 0:2048])
        wuv_tok = emit_cast_load(p, ctx, wuv_d, WUV, 2, D, XSTG[:, 0:2048])
        WS = YF
        wkr = wkT_d.rearrange("(c p) m -> c p m", p=128)
        wvr = wvT_d.rearrange("(c p) m -> c p m", p=128)
        ws_tok = None
        for k in range(NCH):
            bufa = XSTG[:, 0:1024]
            bufb = XSTG[:, 1024:2048]
            g = [ws_tok] if ws_tok else [(sa, sa.v)]
            p.on("sync", lambda e, o=bufa, i=wkr[k]: e.dma_start(out=o, in_=i),
                 wait=g, inc=sl.inc(16))
            p.on("sync", lambda e, o=bufb, i=wvr[k]: e.dma_start(out=o, in_=i),
                 inc=sl.inc(16))
            p.on("vector", lambda e, o=WS[:, k * D:(k + 1) * D], a=bufa, b=bufb:
                 e.tensor_add(o, a, b), wait=[sl.tok()], inc=sd.inc())
            ws_tok = sd.tok()
        for dc in range(NCH):
            bank = pbank[dc % 2]
            for m in range(NCH):
                p.on("tensor", lambda e, o=bank.ap[:, 0:256],
                     l=WS[:, m * D + dc * 128: m * D + (dc + 1) * 128],
                     r=WDNb[:, m * 256:(m + 1) * 256],
                     s_=(m == 0), e_=(m == NCH - 1):
                     e.matmul(o, l, r, start=s_, stop=e_),
                     wait=([ws_tok, wdn_tok] + bank.free) if m == 0 else [], inc=sp.inc())
            p.on("vector", lambda e, o=WKVD[:, dc * 256:(dc + 1) * 256], i=bank.ap[:, 0:256]:
                 e.tensor_copy(o, i), wait=[sp.tok()], inc=sd.inc())
            bank.free = [sd.tok()]
        wkvd_tok = (sd, sd.v)

        def ln_latent(src_re, udst, ustride, lat_off):
            for tt in range(T // 512):
                t0 = tt * 512
                p.on("sync", lambda e, o=fm3(XSTG[:, :], 512), i=src_re[:, :, t0:t0 + 512]:
                     e.dma_start(out=o, in_=i), wait=[(sd, sd.v), (sa, sa.v), (sp, sp.v)],
                     inc=sl.inc(16))
                x_tok = (sl, sl.v)
                u0 = (t0 if udst is U else 0)
                ur = emit_ln_tile(p, ctx, XSTG[:, :], 512, 0, 512,
                                  udst[:, :], u0, ustride,
                                  extra_wait=[x_tok, ctx["const_tok"]],
                                  u_guard=[(sp, sp.v)])
                for c2 in range(2):
                    bank = pbank[c2 % 2]
                    for k in range(NCH):
                        p.on("tensor", lambda e, o=bank.ap[:, :],
                             l=WKVD[:, k * 256 + c2 * 128: k * 256 + (c2 + 1) * 128],
                             r=udst[:, k * ustride + u0: k * ustride + u0 + 512],
                             s_=(k == 0), e_=(k == NCH - 1):
                             e.matmul(o, l, r, start=s_, stop=e_),
                             wait=([ur, wkvd_tok] + bank.free) if k == 0 else [],
                             inc=sp.inc())
                    p.on("vector", lambda e, o=LAT[:, c2 * 4096 + lat_off + t0:
                                                   c2 * 4096 + lat_off + t0 + 512],
                         i=bank.ap[:, :]: e.tensor_copy(o, i),
                         wait=[sp.tok()], inc=sd.inc())
                    bank.free = [sd.tok()]
            return (sd, sd.v)

        xr = rearr_fm(x_d)
        xor_ = rearr_fm(xo_d)
        ln_latent(xor_, UTMP, 512, T)
        lat_tok = ln_latent(xr, U, T, 0)

        bc_toks = []
        for i, BC in enumerate((BCm, BCp, BCr)):
            p.on("sync", lambda e, o=BC[:, :], i_=bcrows_d[i:i + 1, :].partition_broadcast(128)[:, 0, :]:
                 e.dma_start(out=o, in_=i_), wait=[wkvd_tok] if i == 0 else [],
                 inc=sl.inc(16))
            bc_toks.append((sl, sl.v))
        bc_tok = bc_toks[-1]

        for hp in range(NCH):
            for kt8 in range(8):
                bank = pbank[kt8 % 2]
                for dl in range(2):
                    p.on("tensor", lambda e, o=bank.ap[:, :],
                         l=WUK[:, dl * D + hp * 128: dl * D + (hp + 1) * 128],
                         r=LAT[:, dl * 4096 + kt8 * 512: dl * 4096 + (kt8 + 1) * 512],
                         s_=(dl == 0), e_=(dl == 1):
                         e.matmul(o, l, r, start=s_, stop=e_),
                         wait=([lat_tok, wuk_tok, bc_tok] + bank.free) if dl == 0 else [],
                         inc=sp.inc())
                p.on("vector", lambda e, o=KR[:, kt8 * 512:(kt8 + 1) * 512], i=bank.ap[:, :]:
                     e.tensor_copy(o, i), wait=[sp.tok()], inc=sd.inc())
                bank.free = [sd.tok()]
            for kt in range(32):
                bank = pbank[kt % 2]
                for dl in range(2):
                    p.on("tensor", lambda e, o=bank.ap[:, 0:128],
                         l=LAT[:, dl * 4096 + kt * 128: dl * 4096 + (kt + 1) * 128],
                         r=WUV[:, dl * D + hp * 128: dl * D + (hp + 1) * 128],
                         s_=(dl == 0), e_=(dl == 1):
                         e.matmul(o, l, r, start=s_, stop=e_),
                         wait=([wuv_tok] + bank.free) if dl == 0 else [], inc=sp.inc())
                p.on("vector", lambda e, o=VR[:, kt * 128:(kt + 1) * 128], i=bank.ap[:, 0:128]:
                     e.tensor_copy(o, i), wait=[sp.tok()], inc=sd.inc())
                bank.free = [sd.tok()]
            for qt8 in range(4):
                bank = pbank[qt8 % 2]
                for k in range(NCH):
                    p.on("tensor", lambda e, o=bank.ap[:, :],
                         l=WQb[:, k * D + hp * 128: k * D + (hp + 1) * 128],
                         r=U[:, k * T + qt8 * 512: k * T + qt8 * 512 + 512],
                         s_=(k == 0), e_=(k == NCH - 1):
                         e.matmul(o, l, r, start=s_, stop=e_),
                         wait=([wq_tok] + bank.free) if k == 0 else [], inc=sp.inc())
                p.on("scalar", lambda e, o=QH[:, qt8 * 512:(qt8 + 1) * 512], i=bank.ap[:, :]:
                     e.activation(out=o, in_=i, func=AF.Identity, scale=1.0 / 32.0),
                     wait=[sp.tok()], inc=sa.inc())
                bank.free = [sa.tok()]
            qh_tok = (sa, sa.v)
            for (dstq, BC) in ((Qm, BCm), (Qp, BCp), (Qr, BCr)):
                p.on("vector", lambda e, o=dstq[:, :], a=QH[:, :], b=BC[:, :]:
                     e.tensor_mul(o, a, b), wait=[qh_tok, bc_tok, (sd, sd.v)], inc=sd.inc())
            q_tok = (sd, sd.v)

            for qt in range(4):
                q0 = qt * 512
                ostarted = False
                for kt in range(32):
                    own = kt < 16
                    if not own:
                        qsrc, dcol, diag = Qr, kt, None
                    elif kt < 4 * qt:
                        qsrc, dcol, diag = Qm, kt, None
                    elif kt > 4 * qt + 3:
                        qsrc, dcol, diag = Qp, 32 + kt, None
                    else:
                        qsrc, dcol, diag = QH, None, kt - 4 * qt
                    Sb = (S0, S1)[kt % 2]
                    sxs = SX[:, (kt % 2) * 1024:(kt % 2) * 1024 + 1024]
                    p.on("tensor", lambda e, o=Sb.ap[:, 0:512],
                         l=KR[0:64, kt * 128:(kt + 1) * 128], r=qsrc[0:64, q0:q0 + 512]:
                         e.matmul(o, l, r, start=True, stop=True, tile_position=(0, 0)),
                         wait=([q_tok] if (qt == 0 and kt < 2) else []) + Sb.free,
                         inc=sp.inc())
                    p.on("tensor", lambda e, o=Sb.ap[:, 512:1024],
                         l=KR[64:128, kt * 128:(kt + 1) * 128], r=qsrc[64:128, q0:q0 + 512]:
                         e.matmul(o, l, r, start=True, stop=True, tile_position=(64, 0)),
                         inc=sp.inc())
                    s_tok = sp.tok()
                    if diag is None:
                        p.on("scalar", lambda e, o=sxs, i=Sb.ap[:, :], dc_=dcol:
                             e.activation(out=o, in_=i, func=AF.Exp,
                                          scale=dpart[:, dc_:dc_ + 1]),
                             wait=[s_tok], inc=sa.inc())
                        Sb.free = [sa.tok()]
                        sx_tok = sa.tok()
                    else:
                        p.on("vector", lambda e, o=SF[:, 0:512], a=Sb.ap[:, 0:512],
                             b=D2b[:, diag * 512:(diag + 1) * 512]:
                             e.tensor_mul(o, a, b), wait=[s_tok, (sa, sa.v)], inc=sd.inc())
                        p.on("vector", lambda e, o=SF[:, 512:1024], a=Sb.ap[:, 512:1024],
                             b=D2b[:, diag * 512:(diag + 1) * 512]:
                             e.tensor_mul(o, a, b), inc=sd.inc())
                        Sb.free = [sd.tok()]
                        p.on("scalar", lambda e, o=sxs, i=SF[:, :]:
                             e.activation(out=o, in_=i, func=AF.Exp),
                             wait=[sd.tok()], inc=sa.inc())
                        sx_tok = sa.tok()
                    p.on("tensor", lambda e, o=psden.ap[0:1, 0:512], l=ctx["ones_col"],
                         r=sxs[:, 0:1024][:, 0:512], s_=(kt == 0):
                         e.matmul(o, l, r, start=s_, stop=False),
                         wait=[sx_tok] + (psden.free if kt == 0 else []), inc=sp.inc())
                    p.on("tensor", lambda e, o=psden.ap[32:33, 0:512], l=ctx["ones_col"],
                         r=sxs[:, 512:1024], s_=(kt == 0):
                         e.matmul(o, l, r, start=s_, stop=False, tile_position=(0, 32)),
                         inc=sp.inc())
                    p.on("tensor", lambda e, o=psO.ap[0:64, 0:512],
                         l=VR[:, kt * 128: kt * 128 + 64], r=sxs[:, 0:512], s_=(kt == 0):
                         e.matmul(o, l, r, start=s_, stop=False, tile_position=(0, 0)),
                         wait=(psO.free if kt == 0 else []), inc=sp.inc())
                    p.on("tensor", lambda e, o=psO.ap[64:128, 0:512],
                         l=VR[:, kt * 128 + 64: (kt + 1) * 128], r=sxs[:, 512:1024], s_=(kt == 0):
                         e.matmul(o, l, r, start=s_, stop=False, tile_position=(0, 64)),
                         inc=sp.inc())
                att_tok = sp.tok()
                p.on("vector", lambda e: e.tensor_copy(rrow[0:1, 0:512], psden.ap[0:1, 0:512]),
                     wait=[att_tok], inc=sd.inc())
                p.on("scalar", lambda e: e.activation(out=rrow[0:1, 512:1024],
                                                      in_=psden.ap[32:33, 0:512], func=AF.Identity),
                     wait=[att_tok], inc=sa.inc())
                psden.free = [sd.tok(), (sa, sa.v)]
                p.on("vector", lambda e: e.reciprocal(rrow[0:1, 1024:2048], rrow[0:1, 0:1024]),
                     wait=[sd.tok(), (sa, sa.v)], inc=sd.inc())
                rec_tok = sd.tok()
                p.on("vector", lambda e: e.tensor_copy(Osb[:, :], psO.ap[:, 0:512]),
                     wait=[att_tok], inc=sd.inc())
                psO.free = [sd.tok()]
                p.on("tensor", lambda e: e.matmul(psden.ap[0:64, 0:512], onesP[0:1, 0:64],
                                                  rrow[0:1, 1024:1536], start=True, stop=True),
                     wait=[rec_tok] + psden.free, inc=sp.inc())
                p.on("tensor", lambda e: e.matmul(psden.ap[64:128, 0:512], onesP[0:1, 0:64],
                                                  rrow[0:1, 1536:2048], start=True, stop=True,
                                                  tile_position=(0, 64)),
                     inc=sp.inc())
                nb_tok = sp.tok()
                p.on("vector", lambda e, o=YF[:, hp * T + q0: hp * T + q0 + 512]:
                     e.tensor_mul(o, Osb[:, :], psden.ap[:, 0:512]),
                     wait=[nb_tok, (sd, sd.v)], inc=sd.inc())
                psden.free = [sd.tok()]
        yf_tok = (sd, sd.v)

        ctx["cast_stage_free"] = [yf_tok, (sa, sa.v), (sp, sp.v)]
        wo_tok = emit_cast_load(p, ctx, wo_d, WOb, NCH, D, YS[:, 0:2048])
        last_store = None
        for qt in range(4):
            q0 = qt * 512
            g = [yf_tok] + ([last_store] if last_store else [])
            p.on("sync", lambda e, o=fm3(FX[:, :], 512), i=xr[:, :, q0:q0 + 512]:
                 e.dma_start(out=o, in_=i), wait=g + [(sd, sd.v)], inc=sl.inc(16))
            fx_tok = (sl, sl.v)
            for dc in range(NCH):
                bank = pbank[dc % 2]
                for k in range(NCH):
                    p.on("tensor", lambda e, o=bank.ap[:, :],
                         l=WOb[:, k * D + dc * 128: k * D + (dc + 1) * 128],
                         r=YF[:, k * T + q0: k * T + q0 + 512],
                         s_=(k == 0), e_=(k == NCH - 1):
                         e.matmul(o, l, r, start=s_, stop=e_),
                         wait=([yf_tok, wo_tok] + bank.free) if k == 0 else [], inc=sp.inc())
                p.on("vector", lambda e, o=YS[:, dc * 512:(dc + 1) * 512], i=bank.ap[:, :],
                     x_=FX[:, dc * 512:(dc + 1) * 512]:
                     e.scalar_tensor_tensor(out=o, in0=i, scalar=1.0, in1=x_,
                                            op0=ALU.mult, op1=ALU.add),
                     wait=[sp.tok(), fx_tok], inc=sd.inc())
                bank.free = [sd.tok()]
            p.on("sync", lambda e, o=rearr_fm(y_d)[:, :, q0:q0 + 512], i=fm3(YS[:, :], 512):
                 e.dma_start(out=o, in_=i), wait=[(sd, sd.v)], inc=st.inc(16))
            last_store = (st, st.v)
        p.on("gpsimd", lambda e: e.engine_nop(), wait=[(st, st.v)])
        block = es.enter_context(nc.Block())
        p.replay(block)
    return nc


_L0ATTB_NC = None


def run_l0_attn_b(x_full, W_q, W_k, W_v, W_o, W_down, W_up_k, W_up_v):
    global _L0ATTB_NC
    if _L0ATTB_NC is None:
        _L0ATTB_NC = build_l0_attn_b()
    lam = 0.01
    tl = np.arange(T, dtype=np.float64)
    d2 = np.zeros((128, 4 * 512), np.float32)
    for j in range(4):
        pp = np.arange(128)[:, None] + j * 128
        qq = np.arange(512)[None, :]
        d2[:, j * 512:(j + 1) * 512] = np.exp(-lam * np.abs(pp - qq))
    in_maps = []
    for core in range(8):
        b, h = divmod(core, 2)
        xs = np.ascontiguousarray(x_full[b, h * T:(h + 1) * T, :].T)
        xo = np.ascontiguousarray(x_full[b, (1 - h) * T:(2 - h) * T, :].T)
        dpart = np.zeros((128, 64), np.float32)
        for kt in range(16):
            kl = kt * 128 + np.arange(128)
            dpart[:, kt] = np.exp(lam * kl)
            dpart[:, 32 + kt] = np.exp(-lam * kl)
        for kt in range(16, 32):
            kl = (kt - 16) * 128 + np.arange(128)
            if h == 0:
                dpart[:, kt] = np.exp(-lam * (2048.0 + kl))
            else:
                dpart[:, kt] = np.exp(lam * kl - lam * 2048.0)
        bcrows = np.zeros((3, T), np.float32)
        bcrows[0] = np.exp(-lam * tl)
        bcrows[1] = np.exp(lam * tl)
        bcrows[2] = np.exp(lam * tl) if h == 0 else np.exp(-lam * tl)
        in_maps.append({
            "x": xs, "xo": xo, "wq": np.ascontiguousarray(W_q),
            "wkT": np.ascontiguousarray(W_k.T), "wvT": np.ascontiguousarray(W_v.T),
            "wdn": np.ascontiguousarray(W_down), "wuk": np.ascontiguousarray(W_up_k),
            "wuv": np.ascontiguousarray(W_up_v), "wo": np.ascontiguousarray(W_o),
            "dpart": dpart, "bcrows": bcrows, "d2t": d2})
    res = run_bass_kernel_spmd(_L0ATTB_NC, in_maps, core_ids=list(range(8)))
    out = np.empty((B, S, D), np.float32)
    for core in range(8):
        b, h = divmod(core, 2)
        out[b, h * T:(h + 1) * T, :] = res.results[core]["y"].T
    return out



def build_l1_attn_a():
    nc = new_core("l1atta")
    x_d = nc.dram_tensor("x", [D, T], F32, kind="ExternalInput").ap()
    xo_d = nc.dram_tensor("xo", [D, T], F32, kind="ExternalInput").ap()
    wq_d = nc.dram_tensor("wq", [D, D], F32, kind="ExternalInput").ap()
    wk_d = nc.dram_tensor("wk", [D, D], F32, kind="ExternalInput").ap()
    wv_d = nc.dram_tensor("wv", [D, D], F32, kind="ExternalInput").ap()
    wo_d = nc.dram_tensor("wo", [D, D], F32, kind="ExternalInput").ap()
    kaug_d = nc.dram_tensor("kaug", [6, 16 * 4096], F32, kind="ExternalInput").ap()
    qaug_d = nc.dram_tensor("qaug", [6, 16 * 2048], F32, kind="ExternalInput").ap()
    bkh_d = nc.dram_tensor("bkh", [64, 16], F32, kind="ExternalInput").ap()
    bqh_d = nc.dram_tensor("bqh", [64, 16], F32, kind="ExternalInput").ap()
    bvrow_d = nc.dram_tensor("bvrow", [1, D], F32, kind="ExternalInput").ap()
    bo_d = nc.dram_tensor("bo", [128, 8], F32, kind="ExternalInput").ap()
    m2_d = nc.dram_tensor("m2t", [128, 4 * 512], mybir.dt.uint8, kind="ExternalInput").ap()
    y_d = nc.dram_tensor("y", [D, T], F32, kind="ExternalOutput").ap()

    es = contextlib.ExitStack()
    with es:
        U = sb_at(nc, "U", [128, NCH * T], BF16, 0)
        UO = sb_at(nc, "UO", [128, NCH * T], BF16, 32 * KB)
        YF = sb_at(nc, "YF", [128, NCH * T], BF16, 64 * KB)
        WQS = sb_at(nc, "WQS", [128, NCH * 128], BF16, 96 * KB)
        WKS = sb_at(nc, "WKS", [128, NCH * 128], BF16, 98 * KB)
        WVS = sb_at(nc, "WVS", [128, NCH * 128], BF16, 100 * KB)
        WSTG = sb_at(nc, "WSTG", [128, D], F32, 102 * KB)
        KFa = sb_at(nc, "KFa", [70, 4096], BF16, 106 * KB)
        KFb = sb_at(nc, "KFb", [70, 4096], BF16, 114 * KB)
        VT = sb_at(nc, "VT", [128, 4096], BF16, 122 * KB)
        QGa = sb_at(nc, "QGa", [70, T], BF16, 130 * KB)
        QGb = sb_at(nc, "QGb", [70, T], BF16, 134 * KB)
        SX = sb_at(nc, "SX", [128, 2 * 1024], BF16, 138 * KB)
        M2b = sb_at(nc, "M2b", [128, 4 * 512], mybir.dt.uint8, 142 * KB)
        ZRO = sb_at(nc, "ZRO", [128, 512], BF16, 146 * KB)
        SCL = sb_at(nc, "SCL", [128, 512], F32, 144 * KB)
        Osb = sb_at(nc, "Osb", [128, 512], BF16, 147 * KB)
        rrow = sb_at(nc, "rrow", [1, 2048], F32, 148 * KB)
        onesP = sb_at(nc, "onesP", [128, 128], F32, 156 * KB)
        bkh = sb_at(nc, "bkh_sb", [64, 16], F32, 156 * KB + 512)
        bqh = sb_at(nc, "bqh_sb", [64, 16], F32, 156 * KB + 640)
        bvrow = sb_at(nc, "bvrow_sb", [1, D], F32, 156 * KB + 768)
        bo_sb = sb_at(nc, "bo_sbt", [128, 8], F32, 160 * KB + 896)
        AUGS = sb_at(nc, "AUGS", [6, 2048], F32, 161 * KB)
        XSTG = sb_at(nc, "XSTG", [128, NCH * 512], F32, 106 * KB)
        WOb = sb_at(nc, "WOb", [128, NCH * D], BF16, 106 * KB)
        FX = sb_at(nc, "FX", [128, NCH * 512], F32, 122 * KB)
        YS = sb_at(nc, "YS", [128, NCH * 512], F32, 138 * KB)
        FT = sb_at(nc, "FT", [128, 512], F32, 154 * KB)
        ps_A = Bank(es.enter_context(nc.psum_tensor("psA", [128, 512], F32)))
        ps_C = Bank(es.enter_context(nc.psum_tensor("psC", [128, 512], F32)))
        S0 = Bank(es.enter_context(nc.psum_tensor("S0", [128, 1024], F32)))
        S1 = Bank(es.enter_context(nc.psum_tensor("S1", [128, 1024], F32)))
        psO = Bank(es.enter_context(nc.psum_tensor("psO", [128, 512], F32)))
        psden = Bank(es.enter_context(nc.psum_tensor("psden", [128, 512], F32)))
        p = shared_prog(nc, es)
        ctx = make_common(nc, es, p, scr_off=170 * KB)
        sd, sa, sp = ctx["sd"], ctx["sa"], ctx["sp"]
        sl, st, sw = ctx["sl"], ctx["st"], ctx["sw"]
        ctx["ps_A"], ctx["ps_C"] = ps_A, ps_C
        pbank = (psO, psden)

        for (dst, srcd) in ((bkh, bkh_d), (bqh, bqh_d), (bvrow, bvrow_d), (bo_sb, bo_d)):
            p.on("sync", lambda e, o=dst[:, :], i=srcd: e.dma_start(out=o, in_=i), inc=sl.inc(16))
        bconst_tok = (sl, sl.v)
        p.on("sync", lambda e: e.dma_start(out=M2b[:, :], in_=m2_d), inc=sl.inc(16))
        p.on("gpsimd", lambda e: e.memset(onesP[:, :], 1.0), inc=ctx["sg"].inc())
        p.on("gpsimd", lambda e: e.memset(ZRO[:, :], 0.0), inc=ctx["sg"].inc())
        p.on("vector", lambda e: e.tensor_scalar(out=bqh[:, :], in0=bqh[:, :], scalar1=0.125,
                                                 scalar2=None, op0=ALU.mult),
             wait=[bconst_tok, ctx["const_tok"]], inc=sd.inc())
        bq_tok = sd.tok()
        def slice_load(w_dram, dst, hp, guards):
            sl_ = ctx["sl"]
            p.on("sync", lambda e, o=fm3(WSTG[:, :], 128),
                 i=w_dram[:, hp * 128:(hp + 1) * 128].rearrange("(c p) m -> p c m", p=128):
                 e.dma_start(out=o, in_=i), wait=guards, inc=sl_.inc(16))
            p.on("scalar", lambda e, o=dst[:, :], i=WSTG[:, :]:
                 e.activation(out=o, in_=i, func=AF.Identity),
                 wait=[sl_.tok()], inc=sa.inc())
            return (sa, sa.v)

        xr = rearr_fm(x_d)
        xor_ = rearr_fm(xo_d)

        def ln_pass(src_re, udst):
            for tt in range(T // 512):
                t0 = tt * 512
                p.on("sync", lambda e, o=fm3(XSTG[:, :], 512), i=src_re[:, :, t0:t0 + 512]:
                     e.dma_start(out=o, in_=i), wait=[(sd, sd.v), (sa, sa.v), (sp, sp.v)],
                     inc=sl.inc(16))
                x_tok = (sl, sl.v)
                emit_ln_tile(p, ctx, XSTG[:, :], 512, 0, 512, udst[:, :], t0, T,
                             extra_wait=[x_tok, ctx["const_tok"]], u_guard=[(sp, sp.v)])
            return (sd, sd.v)

        ln_pass(xor_, UO)
        u_tok = ln_pass(xr, U)

        for hp in range(NCH):
            hA, hB = 2 * hp, 2 * hp + 1
            wq_tok = slice_load(wq_d, WQS, hp, [(sa, sa.v), (sp, sp.v)])
            wk_tok = slice_load(wk_d, WKS, hp, [(sa, sa.v)])
            wv_tok = slice_load(wv_d, WVS, hp, [(sa, sa.v)])
            for (KF, hh) in ((KFa, hA), (KFb, hB)):
                for half2 in range(2):
                    p.on("sync", lambda e, i=kaug_d[:, hh * 4096 + half2 * 2048:
                                                    hh * 4096 + (half2 + 1) * 2048]:
                         e.dma_start(out=AUGS[:, :], in_=i),
                         wait=[(sa, sa.v), (sp, sp.v)], inc=sl.inc(16))
                    p.on("scalar", lambda e, o=KF[64:70, half2 * 2048:(half2 + 1) * 2048],
                         i=AUGS[0:6, :]:
                         e.activation(out=o, in_=i, func=AF.Identity),
                         wait=[sl.tok()], inc=sa.inc())
            for (QG, hh) in ((QGa, hA), (QGb, hB)):
                p.on("sync", lambda e, i=qaug_d[:, hh * 2048:(hh + 1) * 2048]:
                     e.dma_start(out=AUGS[0:6, 0:2048], in_=i),
                     wait=[(sa, sa.v)], inc=sl.inc(16))
                p.on("scalar", lambda e, o=QG[64:70, :], i=AUGS[0:6, 0:2048]:
                     e.activation(out=o, in_=i, func=AF.Identity),
                     wait=[sl.tok()], inc=sa.inc())
            aug_tok = (sa, sa.v)
            for half, usrc in ((0, U), (1, UO)):
                for t8 in range(4):
                    k0 = half * 2048 + t8 * 512
                    bank = pbank[t8 % 2]
                    for k in range(NCH):
                        p.on("tensor", lambda e, o=bank.ap[:, :],
                             l=WKS[:, k * 128:(k + 1) * 128],
                             r=usrc[:, k * T + t8 * 512: k * T + t8 * 512 + 512],
                             s_=(k == 0), e_=(k == NCH - 1):
                             e.matmul(o, l, r, start=s_, stop=e_),
                             wait=([u_tok, wk_tok, aug_tok] + bank.free) if k == 0 else [],
                             inc=sp.inc())
                    p.on("scalar", lambda e, o=KFa[0:64, k0:k0 + 512], i=bank.ap[0:64, :],
                         bia=bkh[:, hA:hA + 1]:
                         e.activation(out=o, in_=i, func=AF.Identity, bias=bia),
                         wait=[sp.tok()], inc=sa.inc())
                    p.on("scalar", lambda e, o=KFb[0:64, k0:k0 + 512], i=bank.ap[64:128, :],
                         bia=bkh[:, hB:hB + 1]:
                         e.activation(out=o, in_=i, func=AF.Identity, bias=bia),
                         inc=sa.inc())
                    bank.free = [sa.tok()]
            for half, usrc in ((0, U), (1, UO)):
                for kt16 in range(16):
                    kt = half * 16 + kt16
                    bank = pbank[kt % 2]
                    p.on("tensor", lambda e, o=bank.ap[:, 0:128], l=onesP[0:1, 0:128],
                         r=bvrow[0:1, hp * 128:(hp + 1) * 128]:
                         e.matmul(o, l, r, start=True, stop=False),
                         wait=[bconst_tok] + bank.free, inc=sp.inc())
                    for k in range(NCH):
                        p.on("tensor", lambda e, o=bank.ap[:, 0:128],
                             l=usrc[:, k * T + kt16 * 128: k * T + (kt16 + 1) * 128],
                             r=WVS[:, k * 128:(k + 1) * 128],
                             e_=(k == NCH - 1):
                             e.matmul(o, l, r, start=False, stop=e_),
                             wait=[wv_tok] if k == 0 else [], inc=sp.inc())
                    p.on("vector", lambda e, o=VT[:, kt * 128:(kt + 1) * 128], i=bank.ap[:, 0:128]:
                         e.tensor_copy(o, i), wait=[sp.tok()], inc=sd.inc())
                    bank.free = [sd.tok()]
            for t8 in range(4):
                q0 = t8 * 512
                bank = pbank[t8 % 2]
                for k in range(NCH):
                    p.on("tensor", lambda e, o=bank.ap[:, :],
                         l=WQS[:, k * 128:(k + 1) * 128],
                         r=U[:, k * T + q0: k * T + q0 + 512],
                         s_=(k == 0), e_=(k == NCH - 1):
                         e.matmul(o, l, r, start=s_, stop=e_),
                         wait=([wq_tok, bq_tok] + bank.free) if k == 0 else [], inc=sp.inc())
                p.on("scalar", lambda e, o=QGa[0:64, q0:q0 + 512], i=bank.ap[0:64, :],
                     bia=bqh[:, hA:hA + 1]:
                     e.activation(out=o, in_=i, func=AF.Identity, scale=0.125, bias=bia),
                     wait=[sp.tok()], inc=sa.inc())
                p.on("scalar", lambda e, o=QGb[0:64, q0:q0 + 512], i=bank.ap[64:128, :],
                     bia=bqh[:, hB:hB + 1]:
                     e.activation(out=o, in_=i, func=AF.Identity, scale=0.125, bias=bia),
                     inc=sa.inc())
                bank.free = [sa.tok()]
            kvq_tok = (sa, sa.v)
            kvq_tok_d = (sd, sd.v)

            for qt in range(4):
                q0 = qt * 512
                tiles = list(range(0, 4 * qt + 4)) + list(range(16, 32))
                first = True
                for kt in tiles:
                    diag = (kt - 4 * qt) if (4 * qt <= kt <= 4 * qt + 3) else None
                    Sb = (S0, S1)[kt % 2]
                    sxs = SX[:, (kt % 2) * 1024:(kt % 2) * 1024 + 1024]
                    p.on("tensor", lambda e, o=Sb.ap[:, 0:512],
                         l=KFa[0:70, kt * 128:(kt + 1) * 128], r=QGa[0:70, q0:q0 + 512]:
                         e.matmul(o, l, r, start=True, stop=True),
                         wait=([kvq_tok, kvq_tok_d] if (qt == 0 and first) else []) + Sb.free,
                         inc=sp.inc())
                    p.on("tensor", lambda e, o=Sb.ap[:, 512:1024],
                         l=KFb[0:70, kt * 128:(kt + 1) * 128], r=QGb[0:70, q0:q0 + 512]:
                         e.matmul(o, l, r, start=True, stop=True),
                         inc=sp.inc())
                    s_tok = sp.tok()
                    if diag is None:
                        p.on("scalar", lambda e, o=sxs, i=Sb.ap[:, :]:
                             e.activation(out=o, in_=i, func=AF.Exp),
                             wait=[s_tok], inc=sa.inc())
                        Sb.free = [sa.tok()]
                        sx_tok = [sa.tok()]
                    else:
                        for side in range(2):
                            p.on("vector", lambda e, i=Sb.ap[:, side * 512:(side + 1) * 512]:
                                 e.tensor_scalar(out=SCL[:, :], in0=i, scalar1=80.0,
                                                 scalar2=None, op0=ALU.min),
                                 wait=[s_tok] if side == 0 else [(sa, sa.v)], inc=sd.inc())
                            p.on("scalar", lambda e, o=sxs[:, side * 512:(side + 1) * 512],
                                 i=SCL[:, :]:
                                 e.activation(out=o, in_=i, func=AF.Exp),
                                 wait=[sd.tok()], inc=sa.inc())
                        Sb.free = [(sa, sa.v)]
                        p.on("vector", lambda e, o=sxs[:, 0:512], m=M2b[:, diag * 512:(diag + 1) * 512]:
                             e.copy_predicated(o, m, ZRO[:, :]),
                             wait=[(sa, sa.v)], inc=sd.inc())
                        p.on("vector", lambda e, o=sxs[:, 512:1024], m=M2b[:, diag * 512:(diag + 1) * 512]:
                             e.copy_predicated(o, m, ZRO[:, :]),
                             inc=sd.inc())
                        sx_tok = [sd.tok()]
                    p.on("tensor", lambda e, o=psden.ap[0:1, 0:512], l=ctx["ones_col"],
                         r=sxs[:, 0:512], s_=first:
                         e.matmul(o, l, r, start=s_, stop=False),
                         wait=sx_tok + (psden.free if first else []), inc=sp.inc())
                    p.on("tensor", lambda e, o=psden.ap[32:33, 0:512], l=ctx["ones_col"],
                         r=sxs[:, 512:1024], s_=first:
                         e.matmul(o, l, r, start=s_, stop=False, tile_position=(0, 32)),
                         inc=sp.inc())
                    p.on("tensor", lambda e, o=psO.ap[0:64, 0:512],
                         l=VT[:, kt * 128: kt * 128 + 64], r=sxs[:, 0:512], s_=first:
                         e.matmul(o, l, r, start=s_, stop=False, tile_position=(0, 0)),
                         wait=(psO.free if first else []), inc=sp.inc())
                    p.on("tensor", lambda e, o=psO.ap[64:128, 0:512],
                         l=VT[:, kt * 128 + 64: (kt + 1) * 128], r=sxs[:, 512:1024], s_=first:
                         e.matmul(o, l, r, start=s_, stop=False, tile_position=(0, 64)),
                         inc=sp.inc())
                    first = False
                att_tok = sp.tok()
                p.on("vector", lambda e: e.tensor_copy(rrow[0:1, 0:512], psden.ap[0:1, 0:512]),
                     wait=[att_tok], inc=sd.inc())
                p.on("scalar", lambda e: e.activation(out=rrow[0:1, 512:1024],
                                                      in_=psden.ap[32:33, 0:512], func=AF.Identity),
                     wait=[att_tok], inc=sa.inc())
                psden.free = [sd.tok(), (sa, sa.v)]
                p.on("vector", lambda e: e.reciprocal(rrow[0:1, 1024:2048], rrow[0:1, 0:1024]),
                     wait=[sd.tok(), (sa, sa.v)], inc=sd.inc())
                rec_tok = sd.tok()
                p.on("vector", lambda e: e.tensor_copy(Osb[:, :], psO.ap[:, 0:512]),
                     wait=[att_tok], inc=sd.inc())
                psO.free = [sd.tok()]
                p.on("tensor", lambda e: e.matmul(psden.ap[0:64, 0:512], onesP[0:1, 0:64],
                                                  rrow[0:1, 1024:1536], start=True, stop=True),
                     wait=[rec_tok] + psden.free, inc=sp.inc())
                p.on("tensor", lambda e: e.matmul(psden.ap[64:128, 0:512], onesP[0:1, 0:64],
                                                  rrow[0:1, 1536:2048], start=True, stop=True,
                                                  tile_position=(0, 64)),
                     inc=sp.inc())
                nb_tok = sp.tok()
                p.on("vector", lambda e, o=YF[:, hp * T + q0: hp * T + q0 + 512]:
                     e.tensor_mul(o, Osb[:, :], psden.ap[:, 0:512]),
                     wait=[nb_tok, (sd, sd.v)], inc=sd.inc())
                psden.free = [sd.tok()]
        yf_tok = (sd, sd.v)

        ctx["cast_stage_free"] = [yf_tok, (sa, sa.v), (sp, sp.v)]
        wo_tok = emit_cast_load(p, ctx, wo_d, WOb, NCH, D, YS[:, 0:2048])
        last_store = None
        for qt in range(4):
            q0 = qt * 512
            g = [yf_tok] + ([last_store] if last_store else [])
            p.on("sync", lambda e, o=fm3(FX[:, :], 512), i=xr[:, :, q0:q0 + 512]:
                 e.dma_start(out=o, in_=i), wait=g + [(sd, sd.v)], inc=sl.inc(16))
            fx_tok = (sl, sl.v)
            for dc in range(NCH):
                bank = pbank[dc % 2]
                for k in range(NCH):
                    p.on("tensor", lambda e, o=bank.ap[:, :],
                         l=WOb[:, k * D + dc * 128: k * D + (dc + 1) * 128],
                         r=YF[:, k * T + q0: k * T + q0 + 512],
                         s_=(k == 0), e_=(k == NCH - 1):
                         e.matmul(o, l, r, start=s_, stop=e_),
                         wait=([yf_tok, wo_tok] + bank.free) if k == 0 else [], inc=sp.inc())
                p.on("scalar", lambda e, o=FT[:, :], i=bank.ap[:, :], bia=bo_sb[:, dc:dc + 1]:
                     e.activation(out=o, in_=i, func=AF.Identity, bias=bia),
                     wait=[sp.tok(), (sd, sd.v)], inc=sa.inc())
                bank.free = [(sa, sa.v)]
                p.on("vector", lambda e, o=YS[:, dc * 512:(dc + 1) * 512], a=FT[:, :],
                     x_=FX[:, dc * 512:(dc + 1) * 512]:
                     e.tensor_add(o, a, x_),
                     wait=[(sa, sa.v), fx_tok], inc=sd.inc())
            p.on("sync", lambda e, o=rearr_fm(y_d)[:, :, q0:q0 + 512], i=fm3(YS[:, :], 512):
                 e.dma_start(out=o, in_=i), wait=[(sd, sd.v)], inc=st.inc(16))
            last_store = (st, st.v)
        p.on("gpsimd", lambda e: e.engine_nop(), wait=[(st, st.v)])
        block = es.enter_context(nc.Block())
        p.replay(block)
    return nc


_L1ATTA_NC = None


def run_l1_attn_a(x_full, Wq, bq, Wk, bk, Wv, bv, Wo, bo):
    global _L1ATTA_NC
    if _L1ATTA_NC is None:
        _L1ATTA_NC = build_l1_attn_a()
    H_, HD_ = 16, 64
    slopes = (2.0 ** (-8.0 / H_)) ** np.arange(1, H_ + 1, dtype=np.float64)
    BF_ = ml_dtypes.bfloat16

    def split3(a):
        hi = a.astype(BF_).astype(np.float64)
        r = a - hi
        mid = r.astype(BF_).astype(np.float64)
        lo = (r - mid).astype(BF_).astype(np.float64)
        return hi, mid, lo

    m2 = np.zeros((128, 4 * 512), np.uint8)
    for j in range(4):
        pp = np.arange(128)[:, None] + j * 128
        qq = np.arange(512)[None, :]
        m2[:, j * 512:(j + 1) * 512] = (pp > qq).astype(np.uint8)
    bkh = np.ascontiguousarray(bk.reshape(16, 64).T)
    bqh = np.ascontiguousarray(bq.reshape(16, 64).T)
    bvr = bv.reshape(1, D).astype(np.float32)
    bor = np.ascontiguousarray(bo.reshape(8, 128).T)
    in_maps = []
    for core in range(8):
        b, h = divmod(core, 2)
        xs = np.ascontiguousarray(x_full[b, h * T:(h + 1) * T, :].T)
        xo = (np.ascontiguousarray(x_full[b, 0:T, :].T) if h == 1
              else np.zeros((D, T), np.float32))
        kaug = np.zeros((6, 16 * 4096), np.float64)
        kaug[0:3] = 1.0
        qaug = np.zeros((6, 16 * 2048), np.float64)
        qaug[3:6] = 1.0
        for hh in range(16):
            kg = np.concatenate([h * 2048 + np.arange(2048),
                                 np.full(2048, np.nan)])
            if h == 1:
                kg[2048:] = np.arange(2048)
                albk = slopes[hh] * kg
            else:
                albk = slopes[hh] * kg
                albk[2048:] = -1e30
            hi, mid, lo = split3(albk)
            kaug[3, hh * 4096:(hh + 1) * 4096] = hi
            kaug[4, hh * 4096:(hh + 1) * 4096] = mid
            kaug[5, hh * 4096:(hh + 1) * 4096] = lo
            qg = h * 2048 + np.arange(2048)
            aq = -slopes[hh] * qg
            hi, mid, lo = split3(aq)
            qaug[0, hh * 2048:(hh + 1) * 2048] = hi
            qaug[1, hh * 2048:(hh + 1) * 2048] = mid
            qaug[2, hh * 2048:(hh + 1) * 2048] = lo
        in_maps.append({
            "x": xs, "xo": xo, "wq": np.ascontiguousarray(Wq),
            "wk": np.ascontiguousarray(Wk), "wv": np.ascontiguousarray(Wv),
            "wo": np.ascontiguousarray(Wo),
            "kaug": kaug.astype(np.float32), "qaug": qaug.astype(np.float32),
            "bkh": bkh, "bqh": bqh, "bvrow": bvr, "bo": bor, "m2t": m2})
    res = run_bass_kernel_spmd(_L1ATTA_NC, in_maps, core_ids=list(range(8)))
    out = np.empty((B, S, D), np.float32)
    for core in range(8):
        b, h = divmod(core, 2)
        out[b, h * T:(h + 1) * T, :] = res.results[core]["y"].T
    return out




def _emit_exchange(p, ctx, nc0, y_h, need_halo, hflag_ap):
    sd, sl, st = ctx["sd"], ctx["sl"], ctx["st"]
    ph = FUSE["phase"]
    sum_d = nc0.dram_tensor(f"p{ph}_xsum", [D, T], F32, kind="Internal")
    xo_h = nc0.dram_tensor(f"p{ph}_xo", [D, T], F32, kind="Internal")
    FUSE["xo_pre"] = xo_h
    FUSE["halo_pre"] = None
    scc = FUSE.get("cc_sem") or p.sem()
    FUSE["cc_sem"] = scc
    p.barrier()
    p.on("gpsimd", lambda e: e.collective_compute(
        "AllReduce", ALU.add, replica_groups=[[0, 1], [2, 3], [4, 5], [6, 7]],
        ins=[y_h.ap().opt()], outs=[sum_d.ap().opt()]),
        inc=scc.inc(1))
    cc_tok = (scc, scc.v)
    A = sb_at(nc0, f"x{ph}A", [128, NCH * 512], F32, 0)
    Bt = sb_at(nc0, f"x{ph}B", [128, NCH * 512], F32, 16 * KB)
    yr = rearr_fm(y_h.ap())
    sr = rearr_fm(sum_d.ap())
    xr2 = rearr_fm(xo_h.ap())
    last = None
    for tt in range(T // 512):
        t0 = tt * 512
        g = [cc_tok] + ([last] if last else [])
        p.on("sync", lambda e, o=fm3(A[:, :], 512), i=sr[:, :, t0:t0 + 512]:
             e.dma_start(out=o, in_=i), wait=g + [(sd, sd.v)], inc=sl.inc(16))
        p.on("sync", lambda e, o=fm3(Bt[:, :], 512), i=yr[:, :, t0:t0 + 512]:
             e.dma_start(out=o, in_=i), inc=sl.inc(16))
        p.on("vector", lambda e: e.tensor_sub(A[:, :], A[:, :], Bt[:, :]),
             wait=[(sl, sl.v)], inc=sd.inc())
        p.on("sync", lambda e, o=xr2[:, :, t0:t0 + 512], i=fm3(A[:, :], 512):
             e.dma_start(out=o, in_=i), wait=[sd.tok()], inc=st.inc(16))
        last = (st, st.v)
    if need_halo:
        halo_h = nc0.dram_tensor(f"p{ph}_halo", [D, 1], F32, kind="Internal")
        FUSE["halo_pre"] = halo_h
        p.on("sync", lambda e: e.dma_start(
            out=fm3(A[:, 0:8], 1), in_=xr2[:, :, T - 1:T],
            allow_slow_non_contiguous=True),
            wait=[last], inc=sl.inc(16))
        p.on("vector", lambda e: e.tensor_scalar(out=A[:, 8:16], in0=A[:, 0:8],
                                                 scalar1=hflag_ap, scalar2=None, op0=ALU.mult),
             wait=[(sl, sl.v)], inc=sd.inc())
        p.on("sync", lambda e: e.dma_start(
            out=halo_h.ap().rearrange("(c p) t -> p c t", p=128), in_=fm3(A[:, 8:16], 1),
            allow_slow_non_contiguous=True),
            wait=[sd.tok()], inc=st.inc(16))


_FUSED_NC = None
_FUSED_REAL_NC = None


def build_fused():
    global _FUSED_REAL_NC
    nc0 = bass.Bass("TRN2", target_bir_lowering=False, name="mega")
    es = contextlib.ExitStack()
    FUSE.update({"on": True, "phase": 0, "ctx": None, "cc_sem": None})
    FUSE.setdefault("nphases", 8)
    fnc = FusedNC(nc0)
    FUSE["fnc"] = fnc
    with es:
        p = Prog(nc0, es)
        p.defer_replay = True
        FUSE["p"] = p
        x0 = nc0.dram_tensor("p1_x", [D, T], F32, kind="ExternalInput")
        hflag_d = nc0.dram_tensor("hflag", [128, 1], F32, kind="ExternalInput")
        FUSE["chain_in"] = x0
        hfl = sb_at(nc0, "hflag_sb", [128, 1], F32, 207 * KB + 256)

        def phase(i, buildf):
            FUSE["phase"] = i
            buildf()
            FUSE["chain_in"] = FUSE["chain_out"]
            FUSE["xo_pre"] = None
            FUSE["halo_pre"] = None

        def phase_pre(i, need_halo=False):
            FUSE["phase"] = i
            _emit_exchange(p, FUSE["ctx"], nc0, FUSE["chain_out"], need_halo,
                           hfl[:, 0:1])

        FUSE["xo_pre"] = None
        FUSE["halo_pre"] = None
        NP = FUSE["nphases"]
        phase(1, build_l0_attn_a)
        ctx = FUSE["ctx"]
        p.on("sync", lambda e: e.dma_start(out=hfl[:, :], in_=hflag_d.ap()),
             inc=ctx["sl"].inc(16))
        if NP >= 2:
            phase_pre(2, need_halo=True)
            phase(2, build_l0_ffn)
        if NP >= 3:
            phase_pre(3)
            phase(3, build_l0_attn_b)
        if NP >= 4:
            phase_pre(4, need_halo=True)
            phase(4, build_l0_ffn)
        if NP >= 5:
            phase_pre(5)
            phase(5, build_l1_attn_a)
        if NP >= 6:
            phase(6, build_l1_ffn)
        if NP >= 7:
            phase_pre(7)
            phase(7, build_l1_attn_b)
        if NP >= 8:
            phase(8, build_l1_ffn)
        p.defer_replay = False
        with nc0.Block() as block:
            p.replay(block)
    FUSE["on"] = False
    _FUSED_REAL_NC = nc0
    return nc0




def run_fused(ii):
    global _FUSED_NC
    if _FUSED_NC is None:
        _FUSED_NC = build_fused()
    lam = 0.01
    tl = np.arange(T, dtype=np.float64)
    r8 = lambda v: np.ascontiguousarray(v.reshape(8, 128).T)
    p1 = {
        "p1_wq": np.ascontiguousarray(ii["L0_attn_a_wq"]),
        "p1_wk": np.ascontiguousarray(ii["L0_attn_a_wk"]),
        "p1_wv": np.ascontiguousarray(ii["L0_attn_a_wv"]),
        "p1_wo": np.ascontiguousarray(ii["L0_attn_a_wo"]),
        "p1_g1": r8(ii["L0_attn_a_ln1_g"]), "p1_b1": r8(ii["L0_attn_a_ln1_b"]),
        "p1_g2": r8(ii["L0_attn_a_ln2_g"]), "p1_b2": r8(ii["L0_attn_a_ln2_b"]),
        "p1_cw1": np.ascontiguousarray(ii["L0_attn_a_cms_w1"].reshape(3 * D, 4 * D)),
        "p1_cw2": np.ascontiguousarray(ii["L0_attn_a_cms_w2"].reshape(3 * 4 * D, D)),
        "p1_cb1": np.ascontiguousarray(np.concatenate(
            [ii["L0_attn_a_cms_b1"][l].reshape(32, 128).T for l in range(3)], axis=1)),
        "p1_cb2": np.ascontiguousarray(np.concatenate(
            [ii["L0_attn_a_cms_b2"][l].reshape(8, 128).T for l in range(3)], axis=1)),
        "p1_maskt": np.triu(np.ones((128, 128), np.float32)),
    }
    ffn0 = {}
    for ph in (2, 4):
        ffn0[f"p{ph}_wrt"] = np.ascontiguousarray(ii["L0_ffn_W_r"].T)
        ffn0[f"p{ph}_wkt"] = np.ascontiguousarray(ii["L0_ffn_W_k"].T)
        ffn0[f"p{ph}_wvt"] = np.ascontiguousarray(ii["L0_ffn_W_v"].T)
        ffn0[f"p{ph}_mur"] = r8(ii["L0_ffn_mu_r"])
        ffn0[f"p{ph}_muk"] = r8(ii["L0_ffn_mu_k"])
    d2 = np.zeros((128, 4 * 512), np.float32)
    for j in range(4):
        pp = np.arange(128)[:, None] + j * 128
        qq = np.arange(512)[None, :]
        d2[:, j * 512:(j + 1) * 512] = np.exp(-lam * np.abs(pp - qq))
    p3 = {
        "p3_wq": np.ascontiguousarray(ii["L0_attn_b_W_q"]),
        "p3_wkT": np.ascontiguousarray(ii["L0_attn_b_W_k"].T),
        "p3_wvT": np.ascontiguousarray(ii["L0_attn_b_W_v"].T),
        "p3_wdn": np.ascontiguousarray(ii["L0_attn_b_W_down_kv"]),
        "p3_wuk": np.ascontiguousarray(ii["L0_attn_b_W_up_k"]),
        "p3_wuv": np.ascontiguousarray(ii["L0_attn_b_W_up_v"]),
        "p3_wo": np.ascontiguousarray(ii["L0_attn_b_W_o"]),
        "p3_d2t": d2,
    }
    H_ = 16
    slopes = (2.0 ** (-8.0 / H_)) ** np.arange(1, H_ + 1, dtype=np.float64)

    def split3(a):
        hi = a.astype(BF).astype(np.float64)
        r = a - hi
        mid = r.astype(BF).astype(np.float64)
        lo = (r - mid).astype(BF).astype(np.float64)
        return hi, mid, lo

    m2 = np.zeros((128, 4 * 512), np.uint8)
    for j in range(4):
        pp = np.arange(128)[:, None] + j * 128
        qq = np.arange(512)[None, :]
        m2[:, j * 512:(j + 1) * 512] = (pp > qq).astype(np.uint8)
    p5 = {
        "p5_wq": np.ascontiguousarray(ii["L1_attn_a_Wq"]),
        "p5_wk": np.ascontiguousarray(ii["L1_attn_a_Wk"]),
        "p5_wv": np.ascontiguousarray(ii["L1_attn_a_Wv"]),
        "p5_wo": np.ascontiguousarray(ii["L1_attn_a_Wo"]),
        "p5_bkh": np.ascontiguousarray(ii["L1_attn_a_bk"].reshape(16, 64).T),
        "p5_bqh": np.ascontiguousarray(ii["L1_attn_a_bq"].reshape(16, 64).T),
        "p5_bvrow": ii["L1_attn_a_bv"].reshape(1, D).astype(np.float32),
        "p5_bo": r8(ii["L1_attn_a_bo"]),
        "p5_m2t": m2,
    }
    ffn1 = {}
    for ph in (6, 8):
        ffn1[f"p{ph}_w1t"] = np.ascontiguousarray(ii["L1_ffn_W1"].T)
        ffn1[f"p{ph}_w2t"] = np.ascontiguousarray(ii["L1_ffn_W2"].T)
        ffn1[f"p{ph}_b1r"] = np.ascontiguousarray(ii["L1_ffn_b1"].reshape(32, 128).T)
        ffn1[f"p{ph}_b2r"] = np.ascontiguousarray(ii["L1_ffn_b2"].reshape(8, 128).T)
    seg = np.zeros((128, 16 * 32), np.float32)
    for tt in range(16):
        seg[0:64, tt * 32 + 2 * tt] = 1.0 / 64
        seg[64:128, tt * 32 + 2 * tt + 1] = 1.0 / 64
    p7 = {
        "p7_wq": np.ascontiguousarray(ii["L1_attn_b_Wq"]),
        "p7_wk": np.ascontiguousarray(ii["L1_attn_b_Wk"]),
        "p7_wv": np.ascontiguousarray(ii["L1_attn_b_Wv"]),
        "p7_biasr": np.concatenate([ii["L1_attn_b_bq"], ii["L1_attn_b_bk"],
                                    ii["L1_attn_b_bv"]]).reshape(1, 3 * D).astype(np.float32),
        "p7_seg": seg,
        "p7_idn": np.eye(128, dtype=np.float32),
        "p7_ownoff": np.zeros((1, 1), np.float32),
    }
    shared = {}
    for dd in (p1, ffn0, p3, p5, ffn1, p7):
        shared.update(dd)
    x_full = ii["L0_attn_a_x"]
    in_maps = []
    for core in range(8):
        b, h = divmod(core, 2)
        m = dict(shared)
        m["p1_x"] = np.ascontiguousarray(x_full[b, h * T:(h + 1) * T, :].T)
        if h == 0:
            m["p1_xo"] = np.zeros((D, T), np.float32)
            m["p1_vflag"] = np.zeros((128, 1), np.float32)
        else:
            m["p1_xo"] = np.ascontiguousarray(x_full[b, 0:T, :].T)
            m["p1_vflag"] = np.ones((128, 1), np.float32)
        m["hflag"] = np.full((128, 1), float(h), np.float32)
        dpart = np.zeros((128, 64), np.float32)
        for kt in range(16):
            kl = kt * 128 + np.arange(128)
            dpart[:, kt] = np.exp(lam * kl)
            dpart[:, 32 + kt] = np.exp(-lam * kl)
        for kt in range(16, 32):
            kl = (kt - 16) * 128 + np.arange(128)
            if h == 0:
                dpart[:, kt] = np.exp(-lam * (2048.0 + kl))
            else:
                dpart[:, kt] = np.exp(lam * kl - lam * 2048.0)
        bcrows = np.zeros((3, T), np.float32)
        bcrows[0] = np.exp(-lam * tl)
        bcrows[1] = np.exp(lam * tl)
        bcrows[2] = np.exp(lam * tl) if h == 0 else np.exp(-lam * tl)
        m["p3_dpart"] = dpart
        m["p3_bcrows"] = bcrows
        kaug = np.zeros((6, 16 * 4096), np.float64)
        kaug[0:3] = 1.0
        qaug = np.zeros((6, 16 * 2048), np.float64)
        qaug[3:6] = 1.0
        for hh in range(16):
            kg = np.zeros(4096)
            kg[:2048] = h * 2048 + np.arange(2048)
            albk = slopes[hh] * kg
            if h == 1:
                albk[2048:] = slopes[hh] * np.arange(2048)
            else:
                albk[2048:] = -1e30
            hi, mid, lo = split3(albk)
            kaug[3, hh * 4096:(hh + 1) * 4096] = hi
            kaug[4, hh * 4096:(hh + 1) * 4096] = mid
            kaug[5, hh * 4096:(hh + 1) * 4096] = lo
            qg = h * 2048 + np.arange(2048)
            hi, mid, lo = split3(-slopes[hh] * qg)
            qaug[0, hh * 2048:(hh + 1) * 2048] = hi
            qaug[1, hh * 2048:(hh + 1) * 2048] = mid
            qaug[2, hh * 2048:(hh + 1) * 2048] = lo
        m["p5_kaug"] = kaug.astype(np.float32)
        m["p5_qaug"] = qaug.astype(np.float32)
        in_maps.append(m)
    res = run_bass_kernel_spmd(_FUSED_NC, in_maps, core_ids=list(range(8)))
    out = np.empty((B, S, D), np.float32)
    for core in range(8):
        b, h = divmod(core, 2)
        out[b, h * T:(h + 1) * T, :] = res.results[core]["out"].T
    return out


def kernel(**inputs):
    ii = {k: np.asarray(v, np.float32) for k, v in inputs.items()}
    if KERNEL_FUSED:
        return run_fused(ii)
    x = ii["L0_attn_a_x"]
    x = run_l0_attn_a(x, ii["L0_attn_a_ln1_g"], ii["L0_attn_a_ln1_b"],
                      ii["L0_attn_a_wq"], ii["L0_attn_a_wk"], ii["L0_attn_a_wv"],
                      ii["L0_attn_a_wo"], ii["L0_attn_a_ln2_g"], ii["L0_attn_a_ln2_b"],
                      ii["L0_attn_a_cms_w1"], ii["L0_attn_a_cms_b1"],
                      ii["L0_attn_a_cms_w2"], ii["L0_attn_a_cms_b2"])
    x = run_l0_ffn(x, ii["L0_ffn_W_r"], ii["L0_ffn_W_k"], ii["L0_ffn_W_v"],
                   ii["L0_ffn_mu_r"], ii["L0_ffn_mu_k"])
    x = run_l0_attn_b(x, ii["L0_attn_b_W_q"], ii["L0_attn_b_W_k"], ii["L0_attn_b_W_v"],
                      ii["L0_attn_b_W_o"], ii["L0_attn_b_W_down_kv"],
                      ii["L0_attn_b_W_up_k"], ii["L0_attn_b_W_up_v"])
    x = run_l0_ffn(x, ii["L0_ffn_W_r"], ii["L0_ffn_W_k"], ii["L0_ffn_W_v"],
                   ii["L0_ffn_mu_r"], ii["L0_ffn_mu_k"])
    x = run_l1_attn_a(x, ii["L1_attn_a_Wq"], ii["L1_attn_a_bq"], ii["L1_attn_a_Wk"],
                      ii["L1_attn_a_bk"], ii["L1_attn_a_Wv"], ii["L1_attn_a_bv"],
                      ii["L1_attn_a_Wo"], ii["L1_attn_a_bo"])
    x = run_l1_ffn(x, ii["L1_ffn_W1"], ii["L1_ffn_b1"], ii["L1_ffn_W2"], ii["L1_ffn_b2"])
    x = run_l1_attn_b(x, ii["L1_attn_b_Wq"], ii["L1_attn_b_bq"], ii["L1_attn_b_Wk"],
                      ii["L1_attn_b_bk"], ii["L1_attn_b_Wv"], ii["L1_attn_b_bv"])
    x = run_l1_ffn(x, ii["L1_ffn_W1"], ii["L1_ffn_b1"], ii["L1_ffn_W2"], ii["L1_ffn_b2"])
    return x
